# version 20
# baseline (speedup 1.0000x reference)
import numpy as np
import ml_dtypes
from contextlib import ExitStack
import concourse.bass as bass
import concourse.mybir as mybir
from concourse.bass_utils import run_bass_kernel_spmd

F32 = mybir.dt.float32
BF16 = mybir.dt.bfloat16
AF = mybir.ActivationFunctionType
ALU = mybir.AluOpType
AX = mybir.AxisListType

S, D, H, HD, MEM = 4096, 2048, 16, 128, 4
NB, TPB, NT, NCH = 8, 4, 32, 16
NI, INTER = 64, 8192
EPS_LN, EPS_ATT = 1e-5, 1e-6
SCALE = HD ** -0.5

SEM_NAMES = (["pe", "ve", "sc", "po", "dcst", "dmem", "dbw", "dbnc", "dbg",
              "dmoe", "dl2r", "dl2w", "dmw", "dws2", "dwsg", "dweg"]
             + [f"dhb{i}" for i in range(8)]
             + ["dcv_" + n for n in ["wq", "wk", "wv", "wo", "ew", "down"]]
             + [f"{p}{i}" for p in ["dx", "dxr", "dwq", "dws", "dwe", "dup",
                                    "ddn", "dx1t", "dx1w", "dout", "dhs"]
                for i in range(2)]
             + ["dhs2", "dhs3"])


class Builder:
    def __init__(self, nc):
        self.nc = nc
        self.ops = []
        self.ctr = {n: 0 for n in SEM_NAMES}
        self.handles = {}

    def op(self, eng, emit, wait=(), inc=None, amt=1):
        wait = tuple(w for w in wait if w is not None)
        tok = None
        if inc is not None:
            self.ctr[inc] += amt
            tok = (inc, self.ctr[inc])
        self.ops.append((eng, emit, wait, inc, amt))
        return tok

    def dma(self, q, out, in_, sem, wait=(), slow=False):
        return self.op(q, (lambda e, o=out, i=in_, sl=slow: e.dma_start(
            out=o, in_=i, allow_slow_non_contiguous=sl)), wait, inc=sem, amt=16)

    def tok(self, sem):
        return (sem, self.ctr[sem])

    def flush(self, name):
        ops = self.ops
        self.ops = []
        per = {}
        for rec in ops:
            per.setdefault(rec[0], []).append(rec)
        blk_cm = self.nc.Block(name)
        blk_cm.__enter__()
        blk = blk_cm
        for eng_name, recs in per.items():
            def make(recs=recs):
                def f(eng):
                    seen = {}
                    for (_, emit, wait, inc, amt) in recs:
                        for (s, v) in wait:
                            if seen.get(s, 0) < v:
                                eng.wait_ge(self.handles[s], v)
                                seen[s] = v
                        inst = emit(eng)
                        if inc is not None:
                            inst.then_inc(self.handles[inc], amt)
                return f
            getattr(blk, eng_name)(make())
        blk_cm.__exit__(None, None, None)


def build_nc(n_blocks=NB, stop_after=None, dump=False):
    nc = bass.Bass("TRN2", target_bir_lowering=False)
    dt_in = lambda n, s: nc.dram_tensor(n, s, F32, kind="ExternalInput").ap()

    x = dt_in("x", [S, D])
    wq_h = dt_in("wq_h", [16, 128, 2048])
    wk_h = dt_in("wk_h", [16, 128, 2048])
    wv_h = dt_in("wv_h", [16, 128, 2048])
    wo_h = dt_in("wo_h", [16, 128, 2048])
    ew_hh = dt_in("ew_hh", [16, 128, 2048])
    up_h = dt_in("up_h", [64, 128, 2048])
    down_hh = dt_in("down_hh", [64, 128, 2048])
    mems_h = dt_in("mems_h", [16, 128, 4, 128])
    mnT_h = dt_in("mnT_h", [16, 128, 4])
    gatew_h = dt_in("gatew_h", [128, 128])
    gate_h = dt_in("gate_h", [16, 1])
    gamma_h = dt_in("gamma_h", [1, 2048])
    beta_h = dt_in("beta_h", [1, 2048])
    maskT_h = dt_in("maskT_h", [128, 128])
    identb_h = nc.dram_tensor("identb_h", [128, 128], BF16, kind="ExternalInput").ap()

    out = nc.dram_tensor("out", [S, D], F32, kind="ExternalOutput").ap()

    scratch_kind = "ExternalOutput" if dump else "Internal"
    dt_bf = lambda n, s: nc.dram_tensor(n, s, BF16, kind="Internal").ap()
    wq_b = dt_bf("wq_b", [16, 128, 2048])
    wk_b = dt_bf("wk_b", [16, 128, 2048])
    wv_b = dt_bf("wv_b", [16, 128, 2048])
    wo_b = dt_bf("wo_b", [16, 128, 2048])
    ew_b = dt_bf("ew_b", [16, 128, 2048])
    down_b = dt_bf("down_b", [64, 128, 2048])
    x1s = nc.dram_tensor("x1s", [S, D], F32, kind=scratch_kind).ap()
    ln2s = nc.dram_tensor("ln2s", [16, 128, S], BF16, kind=scratch_kind).ap()
    moes = nc.dram_tensor("moes", [S, D], BF16, kind=scratch_kind).ap()
    hs = nc.dram_tensor("hs", [NB, 128, NI, 512], BF16, kind=scratch_kind).ap()
    gwf = nc.dram_tensor("gwf", [16, 4], F32, kind="Internal").ap()
    relf = nc.dram_tensor("relf", [4, 16], F32, kind="Internal").ap()
    omgf = nc.dram_tensor("omgf", [16, 1], F32, kind="Internal").ap()
    if dump:
        dbg_st = nc.dram_tensor("dbg_st", [128, 160], F32, kind="ExternalOutput").ap()
        dbg_qp = nc.dram_tensor("dbg_qp", [128, 8192], BF16, kind="ExternalOutput").ap()
        dbg_kpt = nc.dram_tensor("dbg_kpt", [128, 8192], BF16, kind="ExternalOutput").ap()
        dbg_vct = nc.dram_tensor("dbg_vct", [128, 8192], BF16, kind="ExternalOutput").ap()
        dbg_comb = nc.dram_tensor("dbg_comb", [128, 8192], BF16, kind="ExternalOutput").ap()

    xt = x.rearrange("(n p) d -> n p d", p=128)
    x1t = x1s.rearrange("(n p) d -> n p d", p=128)
    outt = out.rearrange("(n p) d -> n p d", p=128)
    moet = moes.rearrange("(n p) d -> n p d", p=128)

    B = Builder(nc)
    with ExitStack() as stk:
        for nm in SEM_NAMES:
            B.handles[nm] = stk.enter_context(nc.semaphore(nm))
        sbuf = lambda n, s, d: stk.enter_context(nc.sbuf_tensor(n, s, d))

        stats_r = sbuf("stats_r", [128, NT], F32)
        stats_nmr = sbuf("stats_nmr", [128, NT], F32)
        gwcol = sbuf("gwcol", [128, 64], F32)
        omgcol = sbuf("omgcol", [128, 16], F32)
        top2 = sbuf("top2", [128, NT], F32)
        identb = sbuf("identb", [128, 128], BF16)
        maskT = sbuf("maskT", [128, 128], F32)
        ones_bf = sbuf("ones_bf", [128, 1], BF16)
        onesf_row = sbuf("onesf_row", [1, 128], F32)
        onesf_col = sbuf("onesf_col", [128, 1], F32)
        mnT_bf = sbuf("mnT_bf", [128, 64], BF16)
        gate_sb = sbuf("gate_sb", [16, 1], F32)
        gatew_b = sbuf("gatew_b", [128, 128], BF16)
        small = sbuf("small", [128, 80], F32)
        scr = sbuf("scr", [128, 2048], BF16)
        pp = stk.enter_context(nc.psum_tensor("pp", [128, 4096], F32))
        bank = lambda k: pp[:, 512 * k:512 * (k + 1)]
        bank_free = {k: None for k in range(8)}
        buf_free = {}

        p0stk = ExitStack()
        p0sbuf = lambda n, s, d: p0stk.enter_context(nc.sbuf_tensor(n, s, d))
        gatew_f = p0sbuf("gatew_f", [128, 128], F32)
        mnT_sb = p0sbuf("mnT_sb", [128, 64], F32)

        B.dma("sync", identb[:, :], identb_h, "dcst")
        B.dma("sync", maskT[:, :], maskT_h, "dcst")
        B.dma("sync", gate_sb[:, :], gate_h, "dcst")
        B.dma("sync", gatew_f[:, :], gatew_h, "dcst")
        for h in range(H):
            B.dma("sync", mnT_sb[:, 4 * h:4 * h + 4], mnT_h[h], "dcst")
        t_id = t_mask = t_gate = t_gw0 = t_mn = B.tok("dcst")
        t_ones = B.op("vector", lambda e: e.memset(ones_bf[:, :], 1.0), inc="ve")
        t_onesr = B.op("vector", lambda e: e.memset(onesf_row[:, :], 1.0), inc="ve")
        t_onesc = B.op("vector", lambda e: e.memset(onesf_col[:, :], 1.0), inc="ve")
        t_mnbf = B.op("vector", lambda e: e.tensor_copy(mnT_bf[:, :], mnT_sb[:, :]),
                      wait=[t_mn], inc="ve")
        t_gwb = B.op("vector", lambda e: e.tensor_copy(gatew_b[:, :], gatew_f[:, :]),
                     wait=[t_gw0], inc="ve")

        conv_tok = {}
        def emit_conv(lst):
            for (nm, src, dst, nch) in lst:
                step = 4 if nch == 16 else 8
                for a in range(0, nch, step):
                    B.dma("gpsimd", dst[a:a + step], src[a:a + step], "dcv_" + nm)
                conv_tok[nm] = B.tok("dcv_" + nm)
        emit_conv([("wq", wq_h, wq_b, 16), ("wk", wk_h, wk_b, 16),
                   ("wv", wv_h, wv_b, 16), ("wo", wo_h, wo_b, 16)])

        xa = [p0sbuf(f"xa{i}", [128, 2048], F32) for i in range(2)]
        lnx_row = p0sbuf("lnx_row", [1, 2048], F32)
        lnx_fm = p0sbuf("lnx_fm", [128, 16], F32)
        lnx_fmb = p0sbuf("lnx_fmb", [128, 16], BF16)
        qsum_fmb = p0sbuf("qsum_fmb", [128, 16], BF16)
        rel_sb = p0sbuf("rel_sb", [4, 16], F32)
        relT = p0sbuf("relT", [16, 4], F32)
        w_sb = p0sbuf("w_sb", [16, 4], F32)


        x_done = {}
        t1_prev = {0: None, 1: None}
        t4_prev = {0: None, 1: None}
        t5_prev = {0: None, 1: None}
        t7_prev = {0: None, 1: None}
        t_lnx_mm = t_rm_last = None
        for T in range(NT):
            xw = list(x_done.get(T - 2, ()))
            BASE = 16 * (T % 2)
            xk = B.dma("sync", xa[T % 2][:, :], xt[T], f"dx{T % 2}", wait=xw)
            xc = xa[T % 2]
            t_sum = B.op("vector", lambda e, xc=xc, BASE=BASE: e.reduce_sum(
                small[:, BASE + 0:BASE + 1], xc[:, :], axis=AX.X), wait=[xk, t1_prev[T % 2]], inc="ve")
            t_sq = B.op("scalar", lambda e, xc=xc, BASE=BASE: e.activation(
                scr[:, :], xc[:, :], AF.Square, accum_out=small[:, BASE + 1:BASE + 2]),
                wait=[xk, t4_prev[T % 2]], inc="sc")
            t1 = B.op("vector", lambda e, BASE=BASE: e.tensor_scalar(
                small[:, BASE + 2:BASE + 3], small[:, BASE + 0:BASE + 1], 1.0 / D, None, ALU.mult),
                wait=[t_sum, t7_prev[T % 2]], inc="ve")
            t2 = B.op("vector", lambda e, BASE=BASE: e.tensor_scalar(
                small[:, BASE + 3:BASE + 4], small[:, BASE + 1:BASE + 2], 1.0 / D, EPS_LN, ALU.mult, ALU.add),
                wait=[t_sq], inc="ve")
            t3 = B.op("vector", lambda e, BASE=BASE: e.tensor_tensor(
                small[:, BASE + 4:BASE + 5], small[:, BASE + 2:BASE + 3], small[:, BASE + 2:BASE + 3], ALU.mult),
                wait=[t1], inc="ve")
            t4 = B.op("vector", lambda e, BASE=BASE: e.tensor_tensor(
                small[:, BASE + 5:BASE + 6], small[:, BASE + 3:BASE + 4], small[:, BASE + 4:BASE + 5], ALU.subtract),
                wait=[t2, t3, t5_prev[T % 2]], inc="ve")
            t5 = B.op("scalar", lambda e, BASE=BASE: e.activation(
                small[:, BASE + 6:BASE + 7], small[:, BASE + 5:BASE + 6], AF.Ln),
                wait=[t4], inc="sc")
            t6 = B.op("scalar", lambda e, T=T, BASE=BASE: e.activation(
                stats_r[:, T:T + 1], small[:, BASE + 6:BASE + 7], AF.Exp, scale=-0.5),
                wait=[t5], inc="sc")
            t7 = B.op("vector", lambda e, T=T, BASE=BASE: e.scalar_tensor_tensor(
                stats_nmr[:, T:T + 1], small[:, BASE + 2:BASE + 3], -1.0, stats_r[:, T:T + 1],
                ALU.mult, ALU.mult), wait=[t6], inc="ve")
            t_rm_last = B.op("vector", lambda e, T=T, BASE=BASE: e.tensor_scalar(
                small[:, 40 + T:41 + T], stats_nmr[:, T:T + 1], -1.0, None, ALU.mult),
                wait=[t7], inc="ve")
            for ech in range(4):
                t_lnx_mm = B.op("tensor", lambda e, T=T, ech=ech, BASE=BASE: e.matmul(
                    pp[0:1, 512 * ech:512 * (ech + 1)], stats_r[:, T:T + 1],
                    xa[T % 2][:, 512 * ech:512 * (ech + 1)],
                    start=(T == 0), stop=(T == NT - 1)),
                    wait=[t6, xk], inc="pe" if ech == 3 else None)
            x_done[T] = (t_sum, t_sq, t_lnx_mm)
            t1_prev[T % 2], t4_prev[T % 2], t5_prev[T % 2], t7_prev[T % 2] = t1, t4, t5, t7

        t_rm1 = B.op("vector", lambda e: e.reduce_sum(
            small[:, 10:11], small[:, 40:40 + NT], axis=AX.X), wait=[t_rm_last], inc="ve")
        t_rm2 = B.op("tensor", lambda e: e.matmul(
            bank(5)[0:1, 0:1], small[:, 10:11], onesf_col[:, :], start=True, stop=True),
            wait=[t_rm1, t_onesc], inc="pe")
        t_rm3 = B.op("vector", lambda e: e.tensor_copy(small[0:1, 11:12], bank(5)[0:1, 0:1]),
                     wait=[t_rm2], inc="ve")
        t_lnx = B.op("vector", lambda e: e.tensor_scalar(
            lnx_row[:, :], pp[0:1, 0:2048], small[0:1, 11:12], None, ALU.subtract),
            wait=[t_lnx_mm, t_rm3], inc="ve")
        t_tr = None
        for c in range(16):
            t_tr = B.op("tensor", lambda e, c=c: e.matmul(
                bank(4)[:, c:c + 1], lnx_row[0:1, 128 * c:128 * (c + 1)],
                onesf_row[0:1, 0:1], start=True, stop=True),
                wait=[t_lnx, t_onesr], inc="pe" if c == 15 else None)
        t_lnxfm = B.op("vector", lambda e: e.tensor_copy(lnx_fm[:, :], bank(4)[:, 0:16]),
                       wait=[t_tr], inc="ve")
        t_lnxfmb = B.op("vector", lambda e: e.tensor_copy(lnx_fmb[:, :], lnx_fm[:, :]),
                        wait=[t_lnxfm], inc="ve")
        wqs = [p0sbuf(f"wqs{i}", [128, 2048], BF16) for i in range(2)]
        t_qs_mm = {0: None, 1: None}
        for e_ in range(16):
            bf = e_ % 2
            t_wl = B.dma("sync", wqs[bf][:, :], wq_b[e_], f"dwq{bf}",
                         wait=[conv_tok["wq"], t_qs_mm[bf], t_lnx])
            for c in range(16):
                t_qs_mm[bf] = B.op("tensor", lambda e, e_=e_, c=c, bf=bf: e.matmul(
                    bank(4)[:, 16 + e_:17 + e_], wqs[bf][:, 128 * c:128 * (c + 1)],
                    lnx_fmb[:, c:c + 1], start=(c == 0), stop=(c == 15)),
                    wait=[t_wl, t_lnxfmb], inc="pe" if c == 15 else None)
        t_qsfm = B.op("vector", lambda e: e.tensor_copy(qsum_fmb[:, :], bank(4)[:, 16:32]),
                      wait=[t_qs_mm[0], t_qs_mm[1]], inc="ve")
        t_rel_mm = None
        for h in range(H):
            t_rel_mm = B.op("tensor", lambda e, h=h: e.matmul(
                bank(5)[0:4, 16 + h:17 + h], mnT_bf[:, 4 * h:4 * h + 4],
                qsum_fmb[:, h:h + 1], start=True, stop=True),
                wait=[t_qsfm, t_mnbf], inc="pe" if h == 15 else None)
        t_rel = B.op("vector", lambda e: e.tensor_scalar(
            rel_sb[:, :], bank(5)[0:4, 16:32], SCALE / S, None, ALU.mult),
            wait=[t_rel_mm], inc="ve")
        t_reld = B.dma("scalar", relf, rel_sb[:, :], "dbw", wait=[t_rel])
        t_relT = B.dma("sync", relT[:, :], relf.rearrange("m h -> h m"), "dbnc",
                       wait=[t_reld], slow=True)
        t_mx = B.op("vector", lambda e: e.reduce_max(small[0:16, 29:30], relT[:, :], axis=AX.X),
                    wait=[t_relT], inc="ve")
        t_nmx = B.op("vector", lambda e: e.tensor_scalar(
            small[0:16, 30:31], small[0:16, 29:30], -1.0, None, ALU.mult), wait=[t_mx], inc="ve")
        t_ex = B.op("scalar", lambda e: e.activation(
            w_sb[:, :], relT[:, :], AF.Exp, bias=small[0:16, 30:31]), wait=[t_nmx], inc="sc")
        t_sm = B.op("vector", lambda e: e.reduce_sum(small[0:16, 31:32], w_sb[:, :], axis=AX.X),
                    wait=[t_ex], inc="ve")
        t_rs = B.op("vector", lambda e: e.reciprocal(small[0:16, 32:33], small[0:16, 31:32]),
                    wait=[t_sm], inc="ve")
        t_g = B.op("scalar", lambda e: e.activation(small[0:16, 33:34], gate_sb[:, :], AF.Sigmoid),
                   wait=[t_gate, t_ex], inc="sc")
        t_w = B.op("vector", lambda e: e.tensor_scalar(
            w_sb[:, :], w_sb[:, :], small[0:16, 32:33], None, ALU.mult),
            wait=[t_rs], inc="ve")
        t_gw = B.op("vector", lambda e: e.tensor_scalar(
            w_sb[:, :], w_sb[:, :], small[0:16, 33:34], None, ALU.mult),
            wait=[t_w, t_g], inc="ve")
        t_omg = B.op("vector", lambda e: e.tensor_scalar(
            small[0:16, 34:35], small[0:16, 33:34], -1.0, 1.0, ALU.mult, ALU.add),
            wait=[t_g], inc="ve")
        t_gwd = B.dma("scalar", gwf, w_sb[:, :], "dbw", wait=[t_gw, t_reld])
        t_omgd = B.dma("scalar", omgf, small[0:16, 34:35], "dbw", wait=[t_omg, t_gwd])
        gw_row = p0sbuf("gw_row", [1, 64], F32)
        omg_row = p0sbuf("omg_row", [1, 16], F32)
        t_gwr = B.dma("sync", gw_row[:, :], gwf.rearrange("a b -> (a b)")[None, :],
                      "dbnc", wait=[t_gwd, t_relT])
        t_omgr = B.dma("sync", omg_row[:, :], omgf.rearrange("a b -> (a b)")[None, :],
                       "dbnc", wait=[t_omgd, t_gwr])
        t_gwbm = B.op("tensor", lambda e: e.matmul(
            bank(5)[:, 64:128], onesf_row[:, :], gw_row[:, :], start=True, stop=True),
            wait=[t_gwr, t_onesr], inc="pe")
        t_omgbm = B.op("tensor", lambda e: e.matmul(
            bank(5)[:, 128:144], onesf_row[:, :], omg_row[:, :], start=True, stop=True),
            wait=[t_omgr, t_onesr], inc="pe")
        t_gwc = B.op("vector", lambda e: e.tensor_copy(gwcol[:, :], bank(5)[:, 64:128]),
                     wait=[t_gwbm], inc="ve")
        t_omgc = B.op("vector", lambda e: e.tensor_copy(omgcol[:, :], bank(5)[:, 128:144]),
                      wait=[t_omgbm], inc="ve")
        if dump:
            B.dma("scalar", dbg_st[:, 0:32], stats_r[:, :], "dbg", wait=[t7], slow=True)
            B.dma("scalar", dbg_st[:, 32:64], stats_nmr[:, :], "dbg", wait=[t7], slow=True)
            B.dma("scalar", dbg_st[:, 64:80], lnx_fm[:, :], "dbg", wait=[t_lnxfm], slow=True)
            B.dma("scalar", dbg_st[0:16, 80:84], w_sb[:, :], "dbg", wait=[t_gw], slow=True)
            B.dma("scalar", dbg_st[:, 84:148], gwcol[:, :], "dbg", wait=[t_gwc], slow=True)
            B.dma("scalar", dbg_st[0:1, 148:149], small[0:1, 11:12], "dbg", wait=[t_rm3], slow=True)
        if dump:
            B.op("sync", lambda e: [e.wait_ge(B.handles["dbg"], B.ctr["dbg"])])
        B.flush("p0")
        p0stk.close()
        p0_drain = [B.tok("dbw"), B.tok("dbnc")] + ([B.tok("dbg")] if dump else [])
        if stop_after == "p0":
            return nc, B

        pastk = ExitStack()
        pasbuf = lambda n, s, d: pastk.enter_context(nc.sbuf_tensor(n, s, d))
        kvst = pasbuf("kvst", [128, H * 128], F32)
        kvst_b = pasbuf("kvst_b", [128, H * 128], BF16)
        zst = pasbuf("zst", [128, H], F32)
        zst_b = pasbuf("zst_b", [128, H], BF16)
        memsb = pasbuf("memsb", [128, H * 512], BF16)
        for h in range(H):
            B.dma("gpsimd", memsb[:, 512 * h:512 * (h + 1)],
                  mems_h[h].rearrange("p m e -> p (m e)"), "dmem", wait=p0_drain)
        t_mem = [B.tok("dmem")] * H
        conv_items = []
        for (nm, csrc, cdst, nch) in [("ew", ew_hh, ew_b, 16), ("down", down_hh, down_b, 64)]:
            step = 4 if nch == 16 else 8
            for a in range(0, nch, step):
                conv_items.append((nm, csrc[a:a + step], cdst[a:a + step]))
        xln = [pasbuf(f"xln{i}", [128, 2048], F32) for i in range(2)]
        xres = [pasbuf(f"xres{i}", [128, 2048], F32) for i in range(2)]
        ln1tm_s = pasbuf("ln1tm", [128, 2048], BF16)
        ln1tm = [ln1tm_s, ln1tm_s]
        ln1fm = pasbuf("ln1fm", [128, 8192], BF16)
        qp = pasbuf("qp", [128, 8192], BF16)
        kpt = pasbuf("kpt", [128, TPB * 2048], BF16)
        vct = pasbuf("vct", [128, TPB * 2048], BF16)
        kpf2 = [pasbuf(f"kpf{i}", [128, 2048], BF16) for i in range(2)]
        comb_t_s = pasbuf("comb_t", [128, 2048], BF16)
        comb_t = [comb_t_s, comb_t_s]
        elu_t_s = pasbuf("elu_t", [128, 512], BF16)
        elu_t = [elu_t_s, elu_t_s]
        elu_e = [pasbuf(f"elu_e{i}", [128, 512], BF16) for i in range(2)]
        x1blk = pasbuf("x1blk", [128, 8192], BF16)
        x1part = [pasbuf(f"x1part{i}", [128, 512], F32) for i in range(2)]
        ln2tm = pasbuf("ln2tm", [128, 2048], BF16)
        ln2fm = pasbuf("ln2fm", [128, 8192], BF16)
        wbufW = [pasbuf(f"wbufW{i}", [128, 2048], BF16) for i in range(3)]
        wbufQ = wbufW
        wbufS = wbufW
        sc_sb = [pasbuf(f"sc_sb{i}", [128, 128], BF16) for i in range(3)]
        lsc = [pasbuf(f"lsc{i}", [128, 128], BF16) for i in range(3)]
        rns_all = [pasbuf(f"rns_all{i}", [128, 64], F32) for i in range(2)]
        macc = [pasbuf(f"macc{i}", [128, 512], BF16) for i in range(3)]
        att_sm2 = [pasbuf(f"att_sm{i}", [128, 16], F32) for i in range(3)]

        state_rdy = {("kv", h): None for h in range(H)}
        state_rdy.update({("z", h): None for h in range(H)})
        state_rdy.update({("cast", j): None for j in range(4)})
        ln1fm_free = None
        ln2fm_free = None
        qkv_free = None
        t_mmQ_prev = {0: None, 1: None, 2: None}
        x1ev_hist = {}
        att_prev_p = {0: None, 1: None, 2: None, 3: None}
        x1s_tok = {}
        ln2s_tok = {}
        t_lnv_prev = t_lns_prev = None

        def emit_ln2(bb, x1ev):
            nonlocal t_lnv_prev, t_lns_prev, ln2fm_free
            t_ln2_last = None
            for t in range(TPB):
                T = TPB * bb + t
                x1_sl = x1blk[:, 2048 * t:2048 * (t + 1)]
                rdy = [x1ev[(t, 1, 0)], x1ev[(t, 1, 1)],
                       x1ev[(t, 0, 0)], x1ev[(t, 0, 1)]]
                t_s2 = B.op("vector", lambda e, x1_sl=x1_sl: e.reduce_sum(
                    small[:, 60:61], x1_sl, axis=AX.X), wait=rdy + [t_lnv_prev], inc="ve")
                t_q2 = B.op("scalar", lambda e, x1_sl=x1_sl: e.activation(
                    scr[:, :], x1_sl, AF.Square, accum_out=small[:, 61:62]),
                    wait=rdy + [t_lnv_prev], inc="sc")
                u1 = B.op("vector", lambda e: e.tensor_scalar(
                    small[:, 62:63], small[:, 60:61], 1.0 / D, None, ALU.mult),
                    wait=[t_s2], inc="ve")
                u2 = B.op("vector", lambda e: e.tensor_scalar(
                    small[:, 63:64], small[:, 61:62], 1.0 / D, EPS_LN, ALU.mult, ALU.add),
                    wait=[t_q2], inc="ve")
                u3 = B.op("vector", lambda e: e.tensor_tensor(
                    small[:, 64:65], small[:, 62:63], small[:, 62:63], ALU.mult),
                    wait=[u1], inc="ve")
                u4 = B.op("vector", lambda e: e.tensor_tensor(
                    small[:, 65:66], small[:, 63:64], small[:, 64:65], ALU.subtract),
                    wait=[u2, u3], inc="ve")
                u5 = B.op("scalar", lambda e: e.activation(
                    small[:, 66:67], small[:, 65:66], AF.Ln),
                    wait=[u4, t_lns_prev], inc="sc")
                u6 = B.op("scalar", lambda e: e.activation(
                    small[:, 67:68], small[:, 66:67], AF.Exp, scale=-0.5),
                    wait=[u5], inc="sc")
                u7 = B.op("vector", lambda e: e.scalar_tensor_tensor(
                    small[:, 68:69], small[:, 62:63], -1.0, small[:, 67:68],
                    ALU.mult, ALU.mult), wait=[u6], inc="ve")
                t_lnv_prev = u7
                t_ln2_h = {}
                for hf in range(2):
                    t_ln2_h[hf] = B.op("scalar", lambda e, x1_sl=x1_sl, hf=hf: e.activation(
                        ln2tm[:, 1024 * hf:1024 * (hf + 1)],
                        x1_sl[:, 1024 * hf:1024 * (hf + 1)], AF.Identity,
                        scale=small[:, 67:68], bias=small[:, 68:69]),
                        wait=[u7, buf_free.get(("ln2tm", hf))], inc="sc")
                t_lns_prev = t_ln2_h[1]
                buf_free[("x1blk", t)] = t_ln2_h[1]
                for g in range(4):
                    bk = 3 if g % 2 == 0 else 7
                    t_trm = None
                    for cc in range(4):
                            c = 4 * g + cc
                            t_trm = B.op("tensor", lambda e, c=c, cc=cc, bk=bk, g=g: e.matmul(
                                bank(bk)[:, 128 * cc:128 * (cc + 1)],
                                ln2tm[:, 128 * c:128 * (c + 1)], identb[:, :],
                                start=True, stop=True),
                                wait=[t_ln2_h[g // 2], bank_free[bk], ln2fm_free],
                                inc="pe" if cc == 3 else None)
                    dst = ln2fm.ap().rearrange("p (c s) -> p c s", s=512)[
                            :, 4 * g:4 * g + 4, 128 * t:128 * (t + 1)]
                    tev = B.op("vector", lambda e, dst=dst, bk=bk: e.tensor_copy(
                            dst, bank(bk).rearrange("p (c s) -> p c s", s=128)),
                            wait=[t_trm], inc="ve")
                    bank_free[bk] = tev
                    if g == 1:
                            buf_free[("ln2tm", 0)] = tev
                    if g == 3:
                            buf_free[("ln2tm", 1)] = tev
                            t_ln2_last = tev
            t_ld = None
            for c in range(16):
                t_ld = B.dma("scalar", ln2s[c][:, 512 * bb:512 * (bb + 1)],
                                 ln2fm[:, 512 * c:512 * (c + 1)], "dl2w", wait=[t_ln2_last])
            ln2fm_free = t_ld
            ln2s_tok[bb] = t_ld

        for b in range(n_blocks):
            if b >= 1:
                take = conv_items[:2] if b < n_blocks - 1 else conv_items
                conv_items = conv_items[len(take):]
                for (nm, csrc, cdst) in take:
                    B.dma("gpsimd", cdst, csrc, "dcv_" + nm)
                    conv_tok[nm] = B.tok("dcv_" + nm)
            t_ln1_all = None
            for t in range(TPB):
                T = TPB * b + t
                xk = B.dma("sync", xln[t % 2][:, :], xt[T], f"dx{t % 2}",
                           wait=[buf_free.get(("xln", t % 2))] + p0_drain)
                t_ln = B.op("scalar", lambda e, t=t, T=T: e.activation(
                    ln1tm[t % 2][:, :], xln[t % 2][:, :], AF.Identity,
                    scale=stats_r[:, T:T + 1], bias=stats_nmr[:, T:T + 1]),
                    wait=[xk, buf_free.get("ln1tm")], inc="sc")
                buf_free[("xln", t % 2)] = t_ln
                for g in range(4):
                    bk = 3 if g % 2 == 0 else 7
                    t_trm = None
                    for cc in range(4):
                        c = 4 * g + cc
                        t_trm = B.op("tensor", lambda e, t=t, c=c, bk=bk, cc=cc: e.matmul(
                            bank(bk)[:, 128 * cc:128 * (cc + 1)],
                            ln1tm[t % 2][:, 128 * c:128 * (c + 1)], identb[:, :],
                            start=True, stop=True),
                            wait=[t_ln, t_id, bank_free[bk], ln1fm_free],
                            inc="pe" if cc == 3 else None)
                    dst = ln1fm.ap().rearrange("p (c s) -> p c s", s=512)[
                        :, 4 * g:4 * g + 4, 128 * t:128 * (t + 1)]
                    tev = B.op("vector", lambda e, dst=dst, bk=bk: e.tensor_copy(
                        dst, bank(bk).rearrange("p (c s) -> p c s", s=128)),
                        wait=[t_trm], inc="ve")
                    bank_free[bk] = tev
                    t_ln1_all = tev
                buf_free["ln1tm"] = t_ln1_all
            if b > 0:
                emit_ln2(b - 1, x1ev_hist[b - 1])

            t_qp_all = None
            for e_ in range(16):
                bf = e_ % 3
                t_wl = B.dma("sync", wbufQ[bf][:, :], wq_b[e_], f"dws{bf}",
                             wait=[conv_tok["wq"], t_mmQ_prev[bf],
                                   buf_free.get(("wbufS", bf))] + p0_drain)
                bk = 2 + bf
                t_mm = None
                for c in range(16):
                    t_mm = B.op("tensor", lambda e, c=c, bk=bk, bf=bf: e.matmul(
                        bank(bk), wbufQ[bf][:, 128 * c:128 * (c + 1)],
                        ln1fm[:, 512 * c:512 * (c + 1)],
                        start=(c == 0), stop=(c == 15)),
                        wait=[t_wl, t_ln1_all, bank_free[bk] if c == 0 else None],
                        inc="pe" if c == 15 else None)
                t_mmQ_prev[bf] = t_mm
                buf_free[("wbufS", bf)] = t_mm
                ebf = e_ % 2
                t_t = B.op("scalar", lambda e, bk=bk, ebf=ebf: e.activation(
                    elu_t[ebf][:, :], bank(bk), AF.Relu, scale=-1.0),
                    wait=[t_mm, buf_free.get(("elu", ebf)), buf_free.get("elu_t")], inc="sc")
                t_e = B.op("scalar", lambda e, ebf=ebf: e.activation(
                    elu_e[ebf][:, :], elu_t[ebf][:, :], AF.Exp, scale=-1.0),
                    wait=[t_t], inc="sc")
                buf_free["elu_t"] = t_e
                t_qp = B.op("vector", lambda e, e_=e_, bk=bk, ebf=ebf: e.scalar_tensor_tensor(
                    qp[:, 512 * e_:512 * (e_ + 1)], bank(bk), 0.0, elu_e[ebf][:, :],
                    ALU.max, ALU.add), wait=[t_e, qkv_free], inc="ve")
                bank_free[bk] = t_qp
                buf_free[("elu", ebf)] = t_qp
                t_qp_all = t_qp

            def tm_proj(wsrc_ap, wsrc_g_ap, wtok, dst_buf, elu):
                ev_last = None
                wsrc_g = wsrc_g_ap
                for eh in range(2):
                    group_last = {}
                    for cp in range(8):
                        bf = cp % 3
                        t_wl = B.dma("sync", wbufS[bf].ap().rearrange("p (a e) -> p a e", e=1024),
                                     wsrc_g[cp][:, :, 1024 * eh:1024 * (eh + 1)],
                                     f"dws{bf}",
                                     wait=[wtok, buf_free.get(("wbufS", bf))] + p0_drain)
                        tm = None
                        for a_ in range(2):
                            c = 2 * cp + a_
                            for t in range(TPB):
                                for n_ in range(2):
                                    bk = 2 * t + n_
                                    is_last_of_c = (t == TPB - 1 and n_ == 1 and a_ == 1)
                                    tm = B.op("tensor", lambda e, a_=a_, t=t, n_=n_, c=c, bk=bk, bf=bf: e.matmul(
                                        bank(bk), ln1fm[:, 512 * c + 128 * t:512 * c + 128 * (t + 1)],
                                        wbufS[bf][:, 1024 * a_ + 512 * n_:1024 * a_ + 512 * (n_ + 1)],
                                        start=(c == 0), stop=(c == 15)),
                                        wait=[t_wl, t_ln1_all,
                                              bank_free[bk] if c == 0 else None],
                                        inc="pe" if (is_last_of_c or c == 15) else None)
                                    if c == 15:
                                        group_last[(t, n_)] = tm
                        buf_free[("wbufS", bf)] = tm
                    for t in range(TPB):
                        for n_ in range(2):
                            bk = 2 * t + n_
                            off = 2048 * t + 1024 * eh + 512 * n_
                            t_mm = group_last[(TPB - 1, 1)]
                            if elu:
                                bf2 = (2 * t + n_) % 2
                                t_t = B.op("scalar", lambda e, bk=bk, bf2=bf2: e.activation(
                                    elu_t[bf2][:, :], bank(bk), AF.Relu, scale=-1.0),
                                    wait=[t_mm, buf_free.get(("elu", bf2)), buf_free.get("elu_t")], inc="sc")
                                t_e = B.op("scalar", lambda e, bf2=bf2: e.activation(
                                    elu_e[bf2][:, :], elu_t[bf2][:, :], AF.Exp, scale=-1.0),
                                    wait=[t_t], inc="sc")
                                buf_free["elu_t"] = t_e
                                tev = B.op("vector", lambda e, off=off, bk=bk, bf2=bf2: e.scalar_tensor_tensor(
                                    dst_buf[:, off:off + 512], bank(bk), 0.0, elu_e[bf2][:, :],
                                    ALU.max, ALU.add), wait=[t_e, qkv_free], inc="ve")
                                buf_free[("elu", bf2)] = tev
                            else:
                                tev = B.op("vector", lambda e, off=off, bk=bk: e.tensor_copy(
                                    dst_buf[:, off:off + 512], bank(bk)),
                                    wait=[t_mm, qkv_free], inc="ve")
                            bank_free[bk] = tev
                            ev_last = tev
                return ev_last

            wk_g = wk_b.rearrange("(g a) p e -> g p a e", a=2)
            wv_g = wv_b.rearrange("(g a) p e -> g p a e", a=2)
            t_kpt_all = tm_proj(None, wk_g, conv_tok["wk"], kpt, elu=True)
            t_vct_all = tm_proj(None, wv_g, conv_tok["wv"], vct, elu=False)

            comb_fm_rdy = None
            last_c8 = None
            for t in range(TPB):
                T = TPB * b + t
                kpf = kpf2[t % 2]
                kpf_ev = None
                for g in range(4):
                    bk = 3 if g % 2 == 0 else 7
                    t_trm = None
                    for hh in range(4):
                        h = 4 * g + hh
                        t_trm = B.op("tensor", lambda e, t=t, h=h, hh=hh, bk=bk: e.matmul(
                            bank(bk)[:, 128 * hh:128 * (hh + 1)],
                            kpt[:, 2048 * t + 128 * h:2048 * t + 128 * (h + 1)], identb[:, :],
                            start=True, stop=True),
                            wait=[t_kpt_all, bank_free[bk],
                                  buf_free.get(("kpf", t % 2)) if (g == 0 and hh == 0) else None],
                            inc="pe" if hh == 3 else None)
                    tev = B.op("vector", lambda e, kpf=kpf, g=g, bk=bk: e.tensor_copy(
                        kpf[:, 512 * g:512 * (g + 1)], bank(bk)),
                        wait=[t_trm], inc="ve")
                    bank_free[bk] = tev
                    kpf_ev = tev
                t_kpf = kpf_ev

                first = (T == 0)
                pre_ev = {}
                for h in range(H):
                    bS = 4 + h % 3
                    kpt_sl = kpt[:, 2048 * t + 128 * h:2048 * t + 128 * (h + 1)]
                    vct_sl = vct[:, 2048 * t + 128 * h:2048 * t + 128 * (h + 1)]
                    t_kv = B.op("tensor", lambda e, kpt_sl=kpt_sl, vct_sl=vct_sl, bS=bS: e.matmul(
                        bank(bS)[:, 0:128], kpt_sl, vct_sl, start=True, stop=True),
                        wait=[t_kpt_all, t_vct_all, bank_free[bS],
                              state_rdy[("cast", h // 4)]], inc=None)
                    t_z = B.op("tensor", lambda e, kpt_sl=kpt_sl, bS=bS: e.matmul(
                        bank(bS)[:, 132:133], kpt_sl, ones_bf[:, :], start=True, stop=True),
                        wait=[t_ones], inc="pe")
                    if first:
                        t_kvu = B.op("vector", lambda e, h=h, bS=bS: e.tensor_copy(
                            kvst[:, 128 * h:128 * (h + 1)], bank(bS)[:, 0:128]),
                            wait=[t_z], inc="ve")
                        t_zu = B.op("vector", lambda e, h=h, bS=bS: e.tensor_copy(
                            zst[:, h:h + 1], bank(bS)[:, 132:133]),
                            wait=[t_z], inc="ve")
                    else:
                        t_kvu = B.op("vector", lambda e, h=h, bS=bS: e.tensor_tensor(
                            kvst[:, 128 * h:128 * (h + 1)], kvst[:, 128 * h:128 * (h + 1)],
                            bank(bS)[:, 0:128], ALU.add), wait=[t_z], inc="ve")
                        t_zu = B.op("vector", lambda e, h=h, bS=bS: e.tensor_tensor(
                            zst[:, h:h + 1], zst[:, h:h + 1], bank(bS)[:, 132:133], ALU.add),
                            wait=[t_z], inc="ve")
                    bank_free[bS] = t_zu
                    pre_ev[h] = (t_kvu, t_zu)

                t_nmm = None
                for h in range(H):
                    t_nmm = B.op("tensor", lambda e, h=h, t=t: e.matmul(
                        bank(3)[:, 4 * h:4 * h + 4],
                        qp[:, 512 * h + 128 * t: 512 * h + 128 * (t + 1)],
                        mnT_bf[:, 4 * h:4 * h + 4], start=True, stop=True),
                        wait=[t_qp_all, t_mnbf, bank_free[3],
                              buf_free.get(("rns", t % 2))],
                        inc="pe" if h == H - 1 else None)
                ra = rns_all[t % 2]
                r1 = B.op("vector", lambda e, ra=ra: e.tensor_scalar(
                    ra[:, :], bank(3)[:, 0:64], EPS_ATT, None, ALU.max),
                    wait=[t_nmm, buf_free.get(("rns", t % 2))], inc="ve")
                bank_free[3] = r1
                r2 = B.op("vector", lambda e, ra=ra: e.reciprocal(ra[:, :], ra[:, :]),
                          wait=[r1], inc="ve")
                t_rns = B.op("vector", lambda e, ra=ra: e.tensor_tensor(
                    ra[:, :], ra[:, :], gwcol[:, :], ALU.mult),
                    wait=[r2, t_gwc], inc="ve")

                grp1 = {}
                for hx in range(H + 2):
                    if hx < H:
                        h = hx
                        si = h % 3
                        bA, bM = si, 4 + si
                        qp_sl = qp[:, 512 * h + 128 * t: 512 * h + 128 * (t + 1)]
                        t_m1 = B.op("tensor", lambda e, qp_sl=qp_sl, h=h, bA=bA: e.matmul(
                            bank(bA), qp_sl, memsb[:, 512 * h:512 * (h + 1)],
                            start=True, stop=True),
                            wait=[t_qp_all, t_mem[h], bank_free[bA]], inc="pe")
                        t_m3 = B.op("tensor", lambda e, kpf=kpf, qp_sl=qp_sl, h=h, bM=bM: e.matmul(
                            bank(bM)[:, 0:128], kpf[:, 128 * h:128 * (h + 1)], qp_sl,
                            start=True, stop=True),
                            wait=[t_kpf, bank_free[bM]], inc="pe")
                        t_scev = B.op("vector", lambda e, h=h, bM=bM: e.tensor_tensor(
                            sc_sb[h % 3][:, :], bank(bM)[:, 0:128], maskT[:, :], ALU.mult),
                            wait=[t_m3, t_mask, buf_free.get(("sc", h % 3))], inc="ve")
                        grp1[h] = (t_m1, t_scev, bA, bM, qp_sl)
                    if hx >= 2:
                        h = hx - 2
                        (t_m1, t_scev, bA, bM, qp_sl) = grp1[h]
                        if not first:
                            B.op("tensor", lambda e, qp_sl=qp_sl, h=h, bM=bM: e.matmul(
                                bank(bM)[:, 128:256], qp_sl, kvst_b[:, 128 * h:128 * (h + 1)],
                                start=True, stop=False),
                                wait=[state_rdy[("kv", h)], t_scev])
                        B.op("tensor", lambda e, h=h, bM=bM, first=first, t=t: e.matmul(
                            bank(bM)[:, 128:256], sc_sb[h % 3][:, :],
                            vct[:, 2048 * t + 128 * h:2048 * t + 128 * (h + 1)],
                            start=first, stop=True), wait=[t_scev, t_vct_all])
                        if not first:
                            B.op("tensor", lambda e, qp_sl=qp_sl, h=h, bM=bM: e.matmul(
                                bank(bM)[:, 264:265], qp_sl, zst_b[:, h:h + 1],
                                start=True, stop=False),
                                wait=[state_rdy[("z", h)]])
                        t_den = B.op("tensor", lambda e, h=h, bM=bM, first=first: e.matmul(
                            bank(bM)[:, 264:265], sc_sb[h % 3][:, :], ones_bf[:, :],
                            start=first, stop=True), wait=[t_ones], inc="pe")
                        asm = att_sm2[h % 3]
                        d1 = B.op("vector", lambda e, asm=asm, bM=bM: e.tensor_scalar(
                            asm[:, 8:9], bank(bM)[:, 264:265], EPS_ATT, None,
                            ALU.max), wait=[t_den] + list(att_prev_p[h % 3] or ()), inc="ve")
                        d2 = B.op("vector", lambda e, asm=asm: e.reciprocal(asm[:, 8:9], asm[:, 8:9]),
                                  wait=[d1], inc="ve")
                        c6 = B.op("vector", lambda e, asm=asm, h=h: e.tensor_tensor(
                            asm[:, 8:9], asm[:, 8:9], omgcol[:, h:h + 1], ALU.mult),
                            wait=[d2, t_omgc], inc="ve")
                        mc = macc[h % 3]
                        c7s = []
                        for m_ in range(4):
                            c7s.append(B.op("scalar", lambda e, ra=ra, mc=mc, bA=bA, h=h, m_=m_: e.activation(
                                mc[:, 128 * m_:128 * (m_ + 1)], bank(bA)[:, 128 * m_:128 * (m_ + 1)],
                                AF.Identity, scale=ra[:, 4 * h + m_:4 * h + m_ + 1]),
                                wait=[t_m1, t_rns, buf_free.get(("acc", h % 3))], inc="sc"))
                        bank_free[bA] = c7s[-1]
                        c9 = B.op("scalar", lambda e, asm=asm, h=h, bM=bM: e.activation(
                            lsc[h % 3][:, :], bank(bM)[:, 128:256], AF.Identity,
                            scale=asm[:, 8:9]),
                            wait=[c6, buf_free.get(("lsc", h % 3))], inc="sc")
                        bank_free[bM] = c9
                        a1 = B.op("vector", lambda e, mc=mc: e.tensor_tensor(
                            mc[:, 0:128], mc[:, 0:128], mc[:, 128:256], ALU.add),
                            wait=[c7s[1]], inc="ve")
                        a2 = B.op("vector", lambda e, mc=mc: e.tensor_tensor(
                            mc[:, 256:384], mc[:, 256:384], mc[:, 384:512], ALU.add),
                            wait=[c7s[3]], inc="ve")
                        a3 = B.op("gpsimd", lambda e, mc=mc: e.tensor_tensor(
                            mc[:, 0:128], mc[:, 0:128], mc[:, 256:384], ALU.add),
                            wait=[a1, a2], inc="po")
                        c10 = B.op("gpsimd", lambda e, t=t, h=h, mc=mc: e.tensor_tensor(
                            comb_t[t % 2][:, 128 * h:128 * (h + 1)], lsc[h % 3][:, :],
                            mc[:, 0:128], ALU.add),
                            wait=[a3, c9, buf_free.get("comb") if h == 0 else None],
                            inc="po")
                        buf_free[("lsc", h % 3)] = c10
                        buf_free[("acc", h % 3)] = c10
                        att_prev_p[h % 3] = (c9,)
                        buf_free[("sc", h % 3)] = t_den
                        last_c8 = c10
                cast_w = [t_den] + [pre_ev[h][0] for h in range(H)] + [pre_ev[15][1]]
                for j in range(4):
                    state_rdy[("cast", j)] = B.op("vector", lambda e, j=j: e.tensor_copy(
                        kvst_b[:, 512 * j:512 * (j + 1)], kvst[:, 512 * j:512 * (j + 1)]),
                        wait=cast_w, inc="ve")
                    for hh in range(4 * j, 4 * j + 4):
                        state_rdy[("kv", hh)] = state_rdy[("cast", j)]
                t_zc = B.op("vector", lambda e: e.tensor_copy(zst_b[:, :], zst[:, :]),
                            wait=cast_w, inc="ve")
                for hh in range(H):
                    state_rdy[("z", hh)] = t_zc
                buf_free[("kpf", t % 2)] = last_c8
                buf_free[("rns", t % 2)] = c7s[-1]
                for g in range(4):
                    bk = 3 if g % 2 == 0 else 7
                    t_trm = None
                    for hh in range(4):
                        h = 4 * g + hh
                        t_trm = B.op("tensor", lambda e, t=t, h=h, hh=hh, bk=bk: e.matmul(
                            bank(bk)[:, 128 * hh:128 * (hh + 1)],
                            comb_t[t % 2][:, 128 * h:128 * (h + 1)], identb[:, :],
                            start=True, stop=True),
                            wait=[last_c8, bank_free[bk], t_vct_all],
                            inc="pe" if hh == 3 else None)
                    dst = ln1fm.ap().rearrange("p (c s) -> p c s", s=512)[
                        :, 4 * g:4 * g + 4, 128 * t:128 * (t + 1)]
                    tev = B.op("vector", lambda e, dst=dst, bk=bk: e.tensor_copy(
                        dst, bank(bk).rearrange("p (c s) -> p c s", s=128)),
                        wait=[t_trm], inc="ve")
                    bank_free[bk] = tev
                    if g == 3:
                        buf_free["comb"] = tev
                        comb_fm_rdy = tev
            qkv_free = last_c8

            if dump and b == 0:
                B.dma("scalar", dbg_qp[:, :], qp[:, :], "dbg", wait=[t_qp_all])
                B.dma("scalar", dbg_kpt[:, :], kpt[:, :], "dbg", wait=[t_kpt_all])
                B.dma("scalar", dbg_vct[:, :], vct[:, :], "dbg", wait=[t_vct_all])
                B.dma("scalar", dbg_comb[:, :], ln1fm[:, :], "dbg", wait=[comb_fm_rdy])

            t_o_mm_last = None
            x1blk_ev = {}
            for eh in range(2):
                group_last = {}
                wo_g = wo_b.rearrange("(g a) p e -> g p a e", a=2)
                for cp in range(8):
                    bf = cp % 3
                    t_wl = B.dma("sync", wbufS[bf].ap().rearrange("p (a e) -> p a e", e=1024),
                                 wo_g[cp][:, :, 1024 * eh:1024 * (eh + 1)],
                                 f"dws{bf}",
                                 wait=[conv_tok["wo"], buf_free.get(("wbufS", bf))])
                    tm = None
                    for a_ in range(2):
                        c = 2 * cp + a_
                        for t in range(TPB):
                            for n_ in range(2):
                                bk = 2 * t + n_
                                is_last = (t == TPB - 1 and n_ == 1 and a_ == 1)
                                tm = B.op("tensor", lambda e, a_=a_, t=t, n_=n_, c=c, bk=bk, bf=bf: e.matmul(
                                    bank(bk), ln1fm[:, 512 * c + 128 * t:512 * c + 128 * (t + 1)],
                                    wbufS[bf][:, 1024 * a_ + 512 * n_:1024 * a_ + 512 * (n_ + 1)],
                                    start=(c == 0), stop=(c == 15)),
                                    wait=[t_wl, comb_fm_rdy, bank_free[bk] if c == 0 else None],
                                    inc="pe" if (is_last or c == 15) else None)
                                if c == 15:
                                    group_last[(t, n_)] = tm
                    buf_free[("wbufS", bf)] = tm
                t_o_mm_last = group_last[(TPB - 1, 1)]
                for t in range(TPB):
                    T = TPB * b + t
                    xrk = B.dma("sync", xres[t % 2][:, :], xt[T], f"dxr{t % 2}",
                                wait=[buf_free.get(("xres", t % 2))] + p0_drain)
                    for n_ in range(2):
                        bk = 2 * t + n_
                        off = 1024 * eh + 512 * n_
                        pi = n_
                        tevf = B.op("vector", lambda e, pi=pi, t=t, off=off, bk=bk: e.scalar_tensor_tensor(
                            x1part[pi][:, :], bank(bk), 1.0,
                            xres[t % 2][:, off:off + 512], ALU.mult, ALU.add),
                            wait=[t_o_mm_last, xrk, buf_free.get(("x1p", pi))], inc="ve")
                        tevb = B.op("vector", lambda e, pi=pi, t=t, off=off: e.tensor_copy(
                            x1blk[:, 2048 * t + off:2048 * t + off + 512],
                            x1part[pi][:, :]),
                            wait=[tevf, buf_free.get(("x1blk", t))], inc="ve")
                        bank_free[bk] = tevf
                        x1blk_ev[(t, eh, n_)] = tevb
                        t_x1d = B.dma("scalar", x1t[T][:, off:off + 512],
                                      x1part[pi][:, :], f"dx1w{pi}", wait=[tevb])
                        buf_free[("x1p", pi)] = t_x1d
                        x1s_tok[(T, eh, n_)] = t_x1d
                    buf_free[("xres", t % 2)] = x1blk_ev[(t, eh, 1)]
            x1ev_hist[b] = dict(x1blk_ev)
            ln1fm_free = t_o_mm_last

        emit_ln2(n_blocks - 1, x1ev_hist[n_blocks - 1])
        x1s_fin = [B.tok("dx1w0"), B.tok("dx1w1")]
        if dump:
            B.op("sync", lambda e: [e.wait_ge(B.handles[s], B.ctr[s]) for s in
                                    ["dbg", "dx1w0", "dx1w1", "dl2w"]])
        B.flush("pa")
        pastk.close()
        pa_drain = x1s_fin + [B.tok("dl2w")] + ([B.tok("dbg")] if dump else [])
        if stop_after == "pa":
            return nc, B

        pb1stk = ExitStack()
        pb1sbuf = lambda n, s, d: pb1stk.enter_context(nc.sbuf_tensor(n, s, d))
        ln2r = pb1sbuf("ln2r", [128, 65536], BF16)
        stage = [pb1sbuf(f"stage{i}", [128, 2048], BF16) for i in range(4)]
        upbuf = [pb1sbuf(f"upbuf{i}", [128, 2048], BF16) for i in range(2)]
        ustage = [pb1sbuf(f"ustage{i}", [128, 512], BF16) for i in range(2)]
        moeblk = pb1sbuf("moeblk", [128, 8192], BF16)
        wbufE = [pb1sbuf(f"wbufE{i}", [128, 2048], BF16) for i in range(2)]

        for c in range(16):
            B.dma("sync", ln2r[:, 4096 * c:4096 * (c + 1)], ln2s[c], "dl2r",
                  wait=[ln2s_tok[n_blocks - 1]] + pa_drain)
        t_l2r_all = B.tok("dl2r")

        t_top2 = {}
        for T in range(NT):
            bk = T % 2
            t_mm = None
            for c in range(16):
                t_mm = B.op("tensor", lambda e, T=T, c=c, bk=bk: e.matmul(
                    bank(bk)[:, 0:8], ln2r[:, 4096 * c + 128 * T:4096 * c + 128 * (T + 1)],
                    gatew_b[:, 8 * c:8 * (c + 1)], start=(c == 0), stop=(c == 15)),
                    wait=[t_l2r_all, t_gwb, bank_free[bk] if c == 0 else None],
                    inc="pe" if c == 15 else None)
            g1 = B.op("vector", lambda e, bk=bk: e.reduce_max(
                small[:, 70:71], bank(bk)[:, 0:8], axis=AX.X),
                wait=[t_mm, t_top2.get(T - 1)], inc="ve")
            g2 = B.op("vector", lambda e, bk=bk: e.tensor_scalar(
                small[:, 71:79], bank(bk)[:, 0:8], small[:, 70:71], None, ALU.is_ge),
                wait=[g1], inc="ve")
            g3 = B.op("vector", lambda e, bk=bk: e.scalar_tensor_tensor(
                small[:, 71:79], small[:, 71:79], -1e30, bank(bk)[:, 0:8],
                ALU.mult, ALU.add), wait=[g2], inc="ve")
            bank_free[bk] = g3
            g4 = B.op("vector", lambda e: e.reduce_max(
                small[:, 79:80], small[:, 71:79], axis=AX.X), wait=[g3], inc="ve")
            t_top2[T] = B.op("vector", lambda e, T=T: e.tensor_tensor(
                top2[:, T:T + 1], small[:, 70:71], small[:, 79:80], ALU.add),
                wait=[g4], inc="ve")

        moes_tok = {}
        for b in range(NB):
            ev_done = {}
            ew_g = ew_b.rearrange("(g a) p e -> g p a e", a=2)
            for eh in range(2):
                group_last = {}
                for cp in range(8):
                    bf = cp % 2
                    t_wl = B.dma("sync", wbufE[bf].ap().rearrange("p (a e) -> p a e", e=1024),
                                 ew_g[cp][:, :, 1024 * eh:1024 * (eh + 1)],
                                 f"dwe{bf}",
                                 wait=[conv_tok["ew"], buf_free.get(("wbufE", bf))] + pa_drain)
                    tm = None
                    for a_ in range(2):
                        c = 2 * cp + a_
                        for t in range(TPB):
                            T = TPB * b + t
                            for n_ in range(2):
                                bk = 2 * t + n_
                                is_last = (t == TPB - 1 and n_ == 1 and a_ == 1)
                                tm = B.op("tensor", lambda e, a_=a_, T=T, n_=n_, c=c, bk=bk, bf=bf: e.matmul(
                                    bank(bk), ln2r[:, 4096 * c + 128 * T:4096 * c + 128 * (T + 1)],
                                    wbufE[bf][:, 1024 * a_ + 512 * n_:1024 * a_ + 512 * (n_ + 1)],
                                    start=(c == 0), stop=(c == 15)),
                                    wait=[t_wl, t_l2r_all, bank_free[bk] if c == 0 else None],
                                    inc="pe" if (is_last or c == 15) else None)
                                if c == 15:
                                    group_last[(t, n_)] = tm
                    buf_free[("wbufE", bf)] = tm
                t_last = group_last[(TPB - 1, 1)]
                for t in range(TPB):
                    T = TPB * b + t
                    for n_ in range(2):
                        bk = 2 * t + n_
                        off = 2048 * t + 1024 * eh + 512 * n_
                        tev = B.op("vector", lambda e, T=T, off=off, bk=bk: e.tensor_scalar(
                            moeblk[:, off:off + 512], bank(bk), top2[:, T:T + 1], None,
                            ALU.mult), wait=[t_last, t_top2[T],
                                             buf_free.get("moeblk")], inc="ve")
                        bank_free[bk] = tev
                        ev_done[(t, eh, n_)] = tev
            t_md = None
            for t in range(TPB):
                T = TPB * b + t
                t_md = B.dma("scalar", moet[T], moeblk[:, 2048 * t:2048 * (t + 1)],
                             "dmw", wait=[ev_done[(t, 0, 0)], ev_done[(t, 0, 1)],
                                          ev_done[(t, 1, 0)], ev_done[(t, 1, 1)]])
                moes_tok[T] = t_md
            buf_free["moeblk"] = t_md

        hs_tok = {b: [] for b in range(NB)}
        relu_done = {}
        stage_dma = {}
        for half in range(2):
            for i in range(NI):
                pbf = i % 2
                t_ul = B.dma("gpsimd", upbuf[pbf][:, :], up_h[i], f"dup{pbf}",
                             wait=[buf_free.get(("upbuf", pbf))] + pa_drain)
                grp = {}
                for c in range(16):
                    for sb4 in range(4):
                        bk = 4 * pbf + sb4
                        blkid = 4 * half + sb4
                        tm = B.op("tensor", lambda e, i=i, c=c, sb4=sb4, bk=bk, pbf=pbf, blkid=blkid: e.matmul(
                            bank(bk), upbuf[pbf][:, 128 * c:128 * (c + 1)],
                            ln2r.ap().rearrange("p (c s) -> p c s", s=4096)[
                                :, c, 512 * blkid:512 * (blkid + 1)],
                            start=(c == 0), stop=(c == 15)),
                            wait=[t_ul, t_l2r_all,
                                  (relu_done.get((pbf, sb4)) or bank_free[bk]) if c == 0 else None],
                            inc="pe" if c == 15 else None)
                        if c == 15:
                            grp[sb4] = tm
                buf_free[("upbuf", pbf)] = grp[3]
                for sb4 in range(4):
                    bk = 4 * pbf + sb4
                    blkid = 4 * half + sb4
                    ubf = sb4 % 2
                    t_r = B.op("scalar", lambda e, bk=bk, ubf=ubf: e.activation(
                        ustage[ubf][:, :], bank(bk), AF.Relu),
                        wait=[grp[3], buf_free.get(("ustage", ubf))], inc="sc")
                    relu_done[(pbf, sb4)] = t_r
                    t_sqr = B.op("vector", lambda e, sb4=sb4, i=i, ubf=ubf: e.tensor_tensor(
                        stage[sb4][:, 512 * (i % 4):512 * (i % 4 + 1)],
                        ustage[ubf][:, :], ustage[ubf][:, :], ALU.mult),
                        wait=[t_r, stage_dma.get(sb4) if i % 4 == 0 else None], inc="ve")
                    buf_free[("ustage", ubf)] = t_sqr
                    if i % 4 == 3:
                        ig = i // 4
                        t_hd = B.dma("scalar", hs[blkid][:, ig * 4:ig * 4 + 4, :],
                                     stage[sb4].ap().rearrange("p (a s) -> p a s", s=512),
                                     f"dhs{sb4}", wait=[t_sqr])
                        stage_dma[sb4] = t_hd
                        hs_tok[blkid].append(t_hd)

        hs_fin = [B.tok(f"dhs{i}") for i in range(4)]
        moes_fin = B.tok("dmw")
        if dump:
            B.op("sync", lambda e: [e.wait_ge(B.handles[s], B.ctr[s]) for s in
                                    ["dmw", "dhs0", "dhs1", "dhs2", "dhs3"]])
        B.flush("pb1")
        pb1stk.close()
        pb1_drain = hs_fin + [moes_fin] + pa_drain
        if stop_after == "pb1":
            return nc, B

        pb2stk = ExitStack()
        pb2sbuf = lambda n, s, d: pb2stk.enter_context(nc.sbuf_tensor(n, s, d))
        hblk = pb2sbuf("hblk", [128, NI * 512], BF16)
        dbuf = [pb2sbuf(f"dbuf{i}", [128, 4096], BF16) for i in range(2)]
        x1tl = [pb2sbuf(f"x1tl{i}", [128, 2048], F32) for i in range(2)]
        moeh = pb2sbuf("moeh", [128, 4096], BF16)
        ybuf2 = [pb2sbuf(f"ybuf{i}", [128, 8192], BF16) for i in range(2)]
        ynb = pb2sbuf("ynb", [128, 2048], F32)
        gamma_bc = pb2sbuf("gamma_bc", [128, 2048], F32)
        beta_bc = pb2sbuf("beta_bc", [128, 2048], F32)
        xbb = pb2sbuf("xbb", [128, 2048], F32)
        outsb = [pb2sbuf(f"outsb{i}", [128, 2048], F32) for i in range(2)]
        gb_rows = pb2sbuf("gb_rows", [128, 2048], F32)
        t_grow = B.dma("sync", gb_rows[0:1, :], gamma_h, "dcst", wait=pb1_drain)

        gb_toks = []
        g_mms = []
        for e_ in range(4):
            tmg = B.op("tensor", lambda e, e_=e_: e.matmul(
                bank(e_), onesf_row[:, :], gb_rows[0:1, 512 * e_:512 * (e_ + 1)],
                start=True, stop=True), wait=[t_grow, t_onesr, bank_free[e_]], inc="pe")
            g_mms.append(tmg)
            tvg = B.op("vector", lambda e, e_=e_: e.tensor_copy(
                gamma_bc[:, 512 * e_:512 * (e_ + 1)], bank(e_)), wait=[tmg] + pb1_drain, inc="ve")
            gb_toks.append(tvg)
        t_brow = B.dma("sync", gb_rows[0:1, :], beta_h, "dcst", wait=g_mms)
        for e_ in range(4):
            tmb = B.op("tensor", lambda e, e_=e_: e.matmul(
                bank(4 + e_), onesf_row[:, :], gb_rows[0:1, 512 * e_:512 * (e_ + 1)],
                start=True, stop=True), wait=[t_brow, bank_free[4 + e_]], inc="pe")
            tvb = B.op("vector", lambda e, e_=e_: e.tensor_copy(
                beta_bc[:, 512 * e_:512 * (e_ + 1)], bank(4 + e_)), wait=[tmb], inc="ve")
            bank_free[4 + e_] = tvb
            bank_free[e_] = gb_toks[e_]
            gb_toks.append(tvb)

        t_out_last = None
        t_lnv_prev = t_lns_prev = None
        chunk_free = {}
        hblk_free = None
        moeh_free = None
        yb_free = {}
        ev_hist = {}
        for b in range(NB):
            hb_toks = {}
            for hc in range(8):
                hb_toks[hc] = B.dma(
                    "sync", hblk[:, 4096 * hc:4096 * (hc + 1)],
                    hs[b][:, 8 * hc:8 * (hc + 1), :].rearrange("p a s -> p (a s)"),
                    f"dhb{hc}", wait=[chunk_free.get(hc)] + hs_fin)
            if b > 0:
                emit_ln3(b - 1, ev_hist[b - 1])
            ev_done = {}
            moes_pb = moes.rearrange("(n p) d -> p n d", p=128)
            for eh in range(2):
                t_moeh = B.dma(
                    "scalar", moeh.ap().rearrange("p (t j) -> p t j", j=1024),
                    moes_pb[:, TPB * b:TPB * (b + 1), 1024 * eh:1024 * (eh + 1)],
                    "dmoe", wait=[moeh_free] + pb1_drain)
                moe_toks = [t_moeh]
                group_last = {}
                down_g = down_b.rearrange("(g a) p e -> g p a e", a=4)
                for ig in range(NI // 4):
                    dbf = ig % 2
                    t_dl = B.dma("sync", dbuf[dbf].ap().rearrange("p (a e) -> p a e", e=1024),
                                 down_g[ig][:, :, 1024 * eh:1024 * (eh + 1)],
                                 f"ddn{dbf}", wait=[conv_tok["down"], buf_free.get(("dbuf", dbf))] + pb1_drain)
                    tm = None
                    for ii in range(4):
                        i = 4 * ig + ii
                        for t in range(TPB):
                            for n_ in range(2):
                                bk = 2 * t + n_
                                is_last = (t == TPB - 1 and n_ == 1 and ii == 3)
                                tm = B.op("tensor", lambda e, i=i, ii=ii, t=t, n_=n_, bk=bk, dbf=dbf: e.matmul(
                                    bank(bk), hblk[:, 512 * i + 128 * t:512 * i + 128 * (t + 1)],
                                    dbuf[dbf][:, 1024 * ii + 512 * n_:1024 * ii + 512 * (n_ + 1)],
                                    start=(i == 0), stop=(i == NI - 1)),
                                    wait=[t_dl, hb_toks[i // 8], bank_free[bk] if i == 0 else None],
                                    inc="pe" if (is_last or i == NI - 1) else None)
                                if i == NI - 1:
                                    group_last[(t, n_)] = tm
                    buf_free[("dbuf", dbf)] = tm
                    if eh == 1 and ig % 2 == 1:
                        chunk_free[ig // 2] = tm
                for t in range(TPB):
                    for n_ in range(2):
                        bk = 2 * t + n_
                        off = 2048 * t + 1024 * eh + 512 * n_
                        tev = B.op("vector", lambda e, off=off, t=t, n_=n_, bk=bk, b=b: e.scalar_tensor_tensor(
                            ybuf2[b % 2][:, off:off + 512], bank(bk), 1.0,
                            moeh[:, 1024 * t + 512 * n_:1024 * t + 512 * (n_ + 1)],
                            ALU.mult, ALU.add),
                            wait=[group_last[(t, n_)], yb_free.get((b % 2, t))] + moe_toks, inc="ve")
                        bank_free[bk] = tev
                        ev_done[(t, eh, n_)] = tev
                moeh_free = ev_done[(TPB - 1, eh, 1)]
            def emit_ln3(bb, ev):
                nonlocal t_lnv_prev, t_lns_prev, t_out_last
                for t in range(TPB):
                    T = TPB * bb + t
                    y_sl = ybuf2[bb % 2][:, 2048 * t:2048 * (t + 1)]
                    rdy = [ev[(t, 0, 0)], ev[(t, 0, 1)],
                           ev[(t, 1, 0)], ev[(t, 1, 1)]]
                    t_s3 = B.op("vector", lambda e, y_sl=y_sl: e.reduce_sum(
                        small[:, 60:61], y_sl, axis=AX.X), wait=rdy + [t_lnv_prev], inc="ve")
                    t_q3 = B.op("scalar", lambda e, y_sl=y_sl: e.activation(
                        scr[:, :], y_sl, AF.Square, accum_out=small[:, 61:62]),
                        wait=rdy + [t_lnv_prev], inc="sc")
                    u1 = B.op("vector", lambda e: e.tensor_scalar(
                        small[:, 62:63], small[:, 60:61], 1.0 / D, None, ALU.mult),
                        wait=[t_s3], inc="ve")
                    u2 = B.op("vector", lambda e: e.tensor_scalar(
                        small[:, 63:64], small[:, 61:62], 1.0 / D, EPS_LN, ALU.mult, ALU.add),
                        wait=[t_q3], inc="ve")
                    u3 = B.op("vector", lambda e: e.tensor_tensor(
                        small[:, 64:65], small[:, 62:63], small[:, 62:63], ALU.mult),
                        wait=[u1], inc="ve")
                    u4 = B.op("vector", lambda e: e.tensor_tensor(
                        small[:, 65:66], small[:, 63:64], small[:, 64:65], ALU.subtract),
                        wait=[u2, u3], inc="ve")
                    u5 = B.op("scalar", lambda e: e.activation(
                        small[:, 66:67], small[:, 65:66], AF.Ln),
                        wait=[u4, t_lns_prev], inc="sc")
                    u6 = B.op("scalar", lambda e: e.activation(
                        small[:, 67:68], small[:, 66:67], AF.Exp, scale=-0.5),
                        wait=[u5], inc="sc")
                    u7 = B.op("vector", lambda e: e.scalar_tensor_tensor(
                        small[:, 68:69], small[:, 62:63], -1.0, small[:, 67:68],
                        ALU.mult, ALU.mult), wait=[u6], inc="ve")
                    t_lnv_prev = u7
                    t_yn = B.op("scalar", lambda e, y_sl=y_sl: e.activation(
                        ynb[:, :], y_sl, AF.Identity,
                        scale=small[:, 67:68], bias=small[:, 68:69]),
                        wait=[u7, buf_free.get("ynb")], inc="sc")
                    t_lns_prev = t_yn
                    yb_free[(bb % 2, t)] = t_yn
                    T_ = TPB * bb + t
                    x1k = B.dma("scalar", x1tl[t % 2][:, :], x1t[T_], f"dx1t{t % 2}",
                                    wait=[buf_free.get(("x1tl", t % 2))] + pb1_drain)
                    t_xb = B.op("gpsimd", lambda e, t=t: e.tensor_tensor(
                        xbb[:, :], x1tl[t % 2][:, :], beta_bc[:, :],
                        ALU.add), wait=[x1k] + gb_toks + [buf_free.get("xbb")],
                        inc="po")
                    buf_free[("x1tl", t % 2)] = t_xb
                    t_t1 = B.op("vector", lambda e: e.tensor_tensor(
                        ynb[:, :], ynb[:, :], gamma_bc[:, :], ALU.mult),
                        wait=[t_yn] + gb_toks, inc="ve")
                    t_o = B.op("vector", lambda e, t=t: e.tensor_tensor(
                        outsb[t % 2][:, :], ynb[:, :], xbb[:, :], ALU.add),
                        wait=[t_t1, t_xb, buf_free.get(("out", t % 2))], inc="ve")
                    buf_free["ynb"] = t_o
                    t_od = B.dma("scalar", outt[T], outsb[t % 2][:, :], f"dout{t % 2}",
                                     wait=[t_o])
                    buf_free[("out", t % 2)] = t_od
                    buf_free["xbb"] = t_o
                    t_out_last = t_od

            ev_hist[b] = dict(ev_done)

        emit_ln3(NB - 1, ev_hist[NB - 1])
        fin0, fin1 = B.tok("dout0"), B.tok("dout1")
        B.op("sync", lambda e, a=fin0, b_=fin1: (
            e.wait_ge(B.handles[a[0]], a[1]), e.wait_ge(B.handles[b_[0]], b_[1])))
        B.flush("pb2")
        pb2stk.close()

    return nc, B


def _prep_inputs(inputs, b):
    f32 = np.float32
    wq = np.asarray(inputs["L0_attn_w_q"], f32)
    wk = np.asarray(inputs["L0_attn_w_k"], f32)
    wv = np.asarray(inputs["L0_attn_w_v"], f32)
    wo = np.asarray(inputs["L0_attn_w_o"], f32)
    ew = np.asarray(inputs["L0_ffn_expert_weight"], f32)
    up = np.asarray(inputs["L0_ffn_up_w"], f32)
    dn = np.asarray(inputs["L0_ffn_down_w"], f32)
    mems = np.asarray(inputs["L0_attn_memories"], f32)
    mn = np.asarray(inputs["L0_attn_memory_norms"], f32)
    gw = np.asarray(inputs["L0_ffn_gate_w"], f32)

    def qlay(w):
        wT = np.ascontiguousarray(w.T)
        return np.ascontiguousarray(
            wT.reshape(16, 128, 16, 128).transpose(2, 1, 0, 3).reshape(16, 128, 2048))

    def clay(w):
        return np.ascontiguousarray(w.T.reshape(16, 128, 2048))

    m = {
        "wq_h": qlay(wq), "wk_h": clay(wk), "wv_h": clay(wv), "wo_h": clay(wo),
        "ew_hh": np.ascontiguousarray(ew.reshape(16, 128, 2048)),
        "up_h": np.ascontiguousarray(
            up.reshape(16, 128, 64, 128).transpose(2, 1, 0, 3).reshape(64, 128, 2048)),
        "down_hh": np.ascontiguousarray(dn.reshape(64, 128, 2048)),
        "mems_h": np.ascontiguousarray(mems.transpose(1, 2, 0, 3)),
        "mnT_h": np.ascontiguousarray(mn.transpose(1, 2, 0)),
        "gatew_h": np.ascontiguousarray(gw.reshape(16, 128, 8).transpose(1, 0, 2).reshape(128, 128)),
        "gate_h": np.asarray(inputs["L0_attn_gate"], f32).reshape(16, 1),
        "gamma_h": np.asarray(inputs["L0_ffn_ln_gamma"], f32).reshape(1, 2048),
        "beta_h": np.asarray(inputs["L0_ffn_ln_beta"], f32).reshape(1, 2048),
        "maskT_h": np.ascontiguousarray(np.triu(np.ones((128, 128), f32))),
        "identb_h": np.eye(128).astype(ml_dtypes.bfloat16),
        "x": np.ascontiguousarray(np.asarray(inputs["L0_attn_hidden_states"], f32)[b]),
    }
    return m


def kernel(**inputs):
    nc, _ = build_nc()
    in_maps = [_prep_inputs(inputs, b) for b in range(8)]
    res = run_bass_kernel_spmd(nc, in_maps, core_ids=list(range(8)))
    return np.stack([res.results[b]["out"] for b in range(8)], axis=0)


# version 21
# speedup vs baseline: 1.0870x; 1.0870x over previous
import numpy as np
import ml_dtypes
from contextlib import ExitStack
import concourse.bass as bass
import concourse.mybir as mybir
from concourse.bass_utils import run_bass_kernel_spmd

F32 = mybir.dt.float32
BF16 = mybir.dt.bfloat16
AF = mybir.ActivationFunctionType
ALU = mybir.AluOpType
AX = mybir.AxisListType

S, D, H, HD, MEM = 4096, 2048, 16, 128, 4
NB, TPB, NT, NCH = 8, 4, 32, 16
NI, INTER = 64, 8192
EPS_LN, EPS_ATT = 1e-5, 1e-6
SCALE = HD ** -0.5

SEM_NAMES = (["pe", "ve", "sc", "po", "dcst", "dmem", "dbw", "dbnc", "dbg",
              "dmoe", "dl2r", "dl2w", "dmw", "dws2", "dwsg", "dweg"]
             + [f"dhb{i}" for i in range(8)]
             + ["dcv_" + n for n in ["wq", "wk", "wv", "wo", "ew", "down"]]
             + [f"{p}{i}" for p in ["dx", "dxr", "dwq", "dws", "dwe", "dup",
                                    "ddn", "dx1t", "dx1w", "dout", "dhs"]
                for i in range(2)]
             + ["dhs2", "dhs3"])


class Builder:
    def __init__(self, nc):
        self.nc = nc
        self.ops = []
        self.ctr = {n: 0 for n in SEM_NAMES}
        self.handles = {}

    def op(self, eng, emit, wait=(), inc=None, amt=1):
        wait = tuple(w for w in wait if w is not None)
        tok = None
        if inc is not None:
            self.ctr[inc] += amt
            tok = (inc, self.ctr[inc])
        self.ops.append((eng, emit, wait, inc, amt))
        return tok

    def dma(self, q, out, in_, sem, wait=(), slow=False):
        return self.op(q, (lambda e, o=out, i=in_, sl=slow: e.dma_start(
            out=o, in_=i, allow_slow_non_contiguous=sl)), wait, inc=sem, amt=16)

    def tok(self, sem):
        return (sem, self.ctr[sem])

    def flush(self, name):
        ops = self.ops
        self.ops = []
        per = {}
        for rec in ops:
            per.setdefault(rec[0], []).append(rec)
        blk_cm = self.nc.Block(name)
        blk_cm.__enter__()
        blk = blk_cm
        for eng_name, recs in per.items():
            def make(recs=recs):
                def f(eng):
                    seen = {}
                    for (_, emit, wait, inc, amt) in recs:
                        for (s, v) in wait:
                            if seen.get(s, 0) < v:
                                eng.wait_ge(self.handles[s], v)
                                seen[s] = v
                        inst = emit(eng)
                        if inc is not None:
                            inst.then_inc(self.handles[inc], amt)
                return f
            getattr(blk, eng_name)(make())
        blk_cm.__exit__(None, None, None)


def build_nc(n_blocks=NB, stop_after=None, dump=False):
    nc = bass.Bass("TRN2", target_bir_lowering=False)
    dt_in = lambda n, s: nc.dram_tensor(n, s, F32, kind="ExternalInput").ap()

    x = dt_in("x", [S, D])
    wq_h = dt_in("wq_h", [16, 128, 2048])
    wk_h = dt_in("wk_h", [16, 128, 2048])
    wv_h = dt_in("wv_h", [16, 128, 2048])
    wo_h = dt_in("wo_h", [16, 128, 2048])
    ew_hh = dt_in("ew_hh", [16, 128, 2048])
    up_h = dt_in("up_h", [64, 128, 2048])
    down_hh = dt_in("down_hh", [64, 128, 2048])
    mems_h = dt_in("mems_h", [16, 128, 4, 128])
    mnT_h = dt_in("mnT_h", [16, 128, 4])
    gatew_h = dt_in("gatew_h", [128, 128])
    gate_h = dt_in("gate_h", [16, 1])
    gamma_h = dt_in("gamma_h", [1, 2048])
    beta_h = dt_in("beta_h", [1, 2048])
    maskT_h = dt_in("maskT_h", [128, 128])
    identb_h = nc.dram_tensor("identb_h", [128, 128], BF16, kind="ExternalInput").ap()

    out = nc.dram_tensor("out", [S, D], F32, kind="ExternalOutput").ap()

    scratch_kind = "ExternalOutput" if dump else "Internal"
    dt_bf = lambda n, s: nc.dram_tensor(n, s, BF16, kind="Internal").ap()
    wq_b = dt_bf("wq_b", [16, 128, 2048])
    wk_b = dt_bf("wk_b", [16, 128, 2048])
    wv_b = dt_bf("wv_b", [16, 128, 2048])
    wo_b = dt_bf("wo_b", [16, 128, 2048])
    ew_b = dt_bf("ew_b", [16, 128, 2048])
    down_b = dt_bf("down_b", [64, 128, 2048])
    x1s = nc.dram_tensor("x1s", [S, D], F32, kind=scratch_kind).ap()
    ln2s = nc.dram_tensor("ln2s", [16, 128, S], BF16, kind=scratch_kind).ap()
    moes = nc.dram_tensor("moes", [S, D], BF16, kind=scratch_kind).ap()
    hs = nc.dram_tensor("hs", [NB, 128, NI, 512], BF16, kind=scratch_kind).ap()
    gwf = nc.dram_tensor("gwf", [16, 4], F32, kind="Internal").ap()
    relf = nc.dram_tensor("relf", [4, 16], F32, kind="Internal").ap()
    omgf = nc.dram_tensor("omgf", [16, 1], F32, kind="Internal").ap()
    if dump:
        dbg_st = nc.dram_tensor("dbg_st", [128, 160], F32, kind="ExternalOutput").ap()
        dbg_qp = nc.dram_tensor("dbg_qp", [128, 8192], BF16, kind="ExternalOutput").ap()
        dbg_kpt = nc.dram_tensor("dbg_kpt", [128, 8192], BF16, kind="ExternalOutput").ap()
        dbg_vct = nc.dram_tensor("dbg_vct", [128, 8192], BF16, kind="ExternalOutput").ap()
        dbg_comb = nc.dram_tensor("dbg_comb", [128, 8192], BF16, kind="ExternalOutput").ap()

    xt = x.rearrange("(n p) d -> n p d", p=128)
    x1t = x1s.rearrange("(n p) d -> n p d", p=128)
    outt = out.rearrange("(n p) d -> n p d", p=128)
    moet = moes.rearrange("(n p) d -> n p d", p=128)

    B = Builder(nc)
    with ExitStack() as stk:
        for nm in SEM_NAMES:
            B.handles[nm] = stk.enter_context(nc.semaphore(nm))
        sbuf = lambda n, s, d: stk.enter_context(nc.sbuf_tensor(n, s, d))

        stats_r = sbuf("stats_r", [128, NT], F32)
        stats_nmr = sbuf("stats_nmr", [128, NT], F32)
        gwcol = sbuf("gwcol", [128, 64], F32)
        omgcol = sbuf("omgcol", [128, 16], F32)
        top2 = sbuf("top2", [128, NT], F32)
        identb = sbuf("identb", [128, 128], BF16)
        maskT = sbuf("maskT", [128, 128], F32)
        ones_bf = sbuf("ones_bf", [128, 1], BF16)
        onesf_row = sbuf("onesf_row", [1, 128], F32)
        onesf_col = sbuf("onesf_col", [128, 1], F32)
        mnT_bf = sbuf("mnT_bf", [128, 64], BF16)
        gate_sb = sbuf("gate_sb", [16, 1], F32)
        gatew_b = sbuf("gatew_b", [128, 128], BF16)
        small = sbuf("small", [128, 80], F32)
        scr = sbuf("scr", [128, 2048], BF16)
        pp = stk.enter_context(nc.psum_tensor("pp", [128, 4096], F32))
        bank = lambda k: pp[:, 512 * k:512 * (k + 1)]
        bank_free = {k: None for k in range(8)}
        buf_free = {}

        p0stk = ExitStack()
        p0sbuf = lambda n, s, d: p0stk.enter_context(nc.sbuf_tensor(n, s, d))
        gatew_f = p0sbuf("gatew_f", [128, 128], F32)
        mnT_sb = p0sbuf("mnT_sb", [128, 64], F32)

        B.dma("sync", identb[:, :], identb_h, "dcst")
        B.dma("sync", maskT[:, :], maskT_h, "dcst")
        B.dma("sync", gate_sb[:, :], gate_h, "dcst")
        B.dma("sync", gatew_f[:, :], gatew_h, "dcst")
        for h in range(H):
            B.dma("sync", mnT_sb[:, 4 * h:4 * h + 4], mnT_h[h], "dcst")
        t_id = t_mask = t_gate = t_gw0 = t_mn = B.tok("dcst")
        t_ones = B.op("vector", lambda e: e.memset(ones_bf[:, :], 1.0), inc="ve")
        t_onesr = B.op("vector", lambda e: e.memset(onesf_row[:, :], 1.0), inc="ve")
        t_onesc = B.op("vector", lambda e: e.memset(onesf_col[:, :], 1.0), inc="ve")
        t_mnbf = B.op("vector", lambda e: e.tensor_copy(mnT_bf[:, :], mnT_sb[:, :]),
                      wait=[t_mn], inc="ve")
        t_gwb = B.op("vector", lambda e: e.tensor_copy(gatew_b[:, :], gatew_f[:, :]),
                     wait=[t_gw0], inc="ve")

        conv_tok = {}
        def emit_conv(lst):
            for (nm, src, dst, nch) in lst:
                step = 4 if nch == 16 else 8
                for a in range(0, nch, step):
                    B.dma("gpsimd", dst[a:a + step], src[a:a + step], "dcv_" + nm)
                conv_tok[nm] = B.tok("dcv_" + nm)
        emit_conv([("wq", wq_h, wq_b, 16), ("wk", wk_h, wk_b, 16),
                   ("wv", wv_h, wv_b, 16), ("wo", wo_h, wo_b, 16)])

        xa = [p0sbuf(f"xa{i}", [128, 2048], F32) for i in range(2)]
        lnx_row = p0sbuf("lnx_row", [1, 2048], F32)
        lnx_fm = p0sbuf("lnx_fm", [128, 16], F32)
        lnx_fmb = p0sbuf("lnx_fmb", [128, 16], BF16)
        qsum_fmb = p0sbuf("qsum_fmb", [128, 16], BF16)
        rel_sb = p0sbuf("rel_sb", [4, 16], F32)
        relT = p0sbuf("relT", [16, 4], F32)
        w_sb = p0sbuf("w_sb", [16, 4], F32)


        x_done = {}
        t1_prev = {0: None, 1: None}
        t4_prev = {0: None, 1: None}
        t5_prev = {0: None, 1: None}
        t7_prev = {0: None, 1: None}
        t_lnx_mm = t_rm_last = None
        for T in range(NT):
            xw = list(x_done.get(T - 2, ()))
            BASE = 16 * (T % 2)
            xk = B.dma("sync", xa[T % 2][:, :], xt[T], f"dx{T % 2}", wait=xw)
            xc = xa[T % 2]
            t_sum = B.op("vector", lambda e, xc=xc, BASE=BASE: e.reduce_sum(
                small[:, BASE + 0:BASE + 1], xc[:, :], axis=AX.X), wait=[xk, t1_prev[T % 2]], inc="ve")
            t_sq = B.op("scalar", lambda e, xc=xc, BASE=BASE: e.activation(
                scr[:, :], xc[:, :], AF.Square, accum_out=small[:, BASE + 1:BASE + 2]),
                wait=[xk, t4_prev[T % 2]], inc="sc")
            t1 = B.op("vector", lambda e, BASE=BASE: e.tensor_scalar(
                small[:, BASE + 2:BASE + 3], small[:, BASE + 0:BASE + 1], 1.0 / D, None, ALU.mult),
                wait=[t_sum, t7_prev[T % 2]], inc="ve")
            t2 = B.op("vector", lambda e, BASE=BASE: e.tensor_scalar(
                small[:, BASE + 3:BASE + 4], small[:, BASE + 1:BASE + 2], 1.0 / D, EPS_LN, ALU.mult, ALU.add),
                wait=[t_sq], inc="ve")
            t3 = B.op("vector", lambda e, BASE=BASE: e.tensor_tensor(
                small[:, BASE + 4:BASE + 5], small[:, BASE + 2:BASE + 3], small[:, BASE + 2:BASE + 3], ALU.mult),
                wait=[t1], inc="ve")
            t4 = B.op("vector", lambda e, BASE=BASE: e.tensor_tensor(
                small[:, BASE + 5:BASE + 6], small[:, BASE + 3:BASE + 4], small[:, BASE + 4:BASE + 5], ALU.subtract),
                wait=[t2, t3, t5_prev[T % 2]], inc="ve")
            t5 = B.op("scalar", lambda e, BASE=BASE: e.activation(
                small[:, BASE + 6:BASE + 7], small[:, BASE + 5:BASE + 6], AF.Ln),
                wait=[t4], inc="sc")
            t6 = B.op("scalar", lambda e, T=T, BASE=BASE: e.activation(
                stats_r[:, T:T + 1], small[:, BASE + 6:BASE + 7], AF.Exp, scale=-0.5),
                wait=[t5], inc="sc")
            t7 = B.op("vector", lambda e, T=T, BASE=BASE: e.scalar_tensor_tensor(
                stats_nmr[:, T:T + 1], small[:, BASE + 2:BASE + 3], -1.0, stats_r[:, T:T + 1],
                ALU.mult, ALU.mult), wait=[t6], inc="ve")
            t_rm_last = B.op("vector", lambda e, T=T, BASE=BASE: e.tensor_scalar(
                small[:, 40 + T:41 + T], stats_nmr[:, T:T + 1], -1.0, None, ALU.mult),
                wait=[t7], inc="ve")
            for ech in range(4):
                t_lnx_mm = B.op("tensor", lambda e, T=T, ech=ech, BASE=BASE: e.matmul(
                    pp[0:1, 512 * ech:512 * (ech + 1)], stats_r[:, T:T + 1],
                    xa[T % 2][:, 512 * ech:512 * (ech + 1)],
                    start=(T == 0), stop=(T == NT - 1)),
                    wait=[t6, xk], inc="pe" if ech == 3 else None)
            x_done[T] = (t_sum, t_sq, t_lnx_mm)
            t1_prev[T % 2], t4_prev[T % 2], t5_prev[T % 2], t7_prev[T % 2] = t1, t4, t5, t7

        t_rm1 = B.op("vector", lambda e: e.reduce_sum(
            small[:, 10:11], small[:, 40:40 + NT], axis=AX.X), wait=[t_rm_last], inc="ve")
        t_rm2 = B.op("tensor", lambda e: e.matmul(
            bank(5)[0:1, 0:1], small[:, 10:11], onesf_col[:, :], start=True, stop=True),
            wait=[t_rm1, t_onesc], inc="pe")
        t_rm3 = B.op("vector", lambda e: e.tensor_copy(small[0:1, 11:12], bank(5)[0:1, 0:1]),
                     wait=[t_rm2], inc="ve")
        t_lnx = B.op("vector", lambda e: e.tensor_scalar(
            lnx_row[:, :], pp[0:1, 0:2048], small[0:1, 11:12], None, ALU.subtract),
            wait=[t_lnx_mm, t_rm3], inc="ve")
        t_tr = None
        for c in range(16):
            t_tr = B.op("tensor", lambda e, c=c: e.matmul(
                bank(4)[:, c:c + 1], lnx_row[0:1, 128 * c:128 * (c + 1)],
                onesf_row[0:1, 0:1], start=True, stop=True),
                wait=[t_lnx, t_onesr], inc="pe" if c == 15 else None)
        t_lnxfm = B.op("vector", lambda e: e.tensor_copy(lnx_fm[:, :], bank(4)[:, 0:16]),
                       wait=[t_tr], inc="ve")
        t_lnxfmb = B.op("vector", lambda e: e.tensor_copy(lnx_fmb[:, :], lnx_fm[:, :]),
                        wait=[t_lnxfm], inc="ve")
        wqs = [p0sbuf(f"wqs{i}", [128, 2048], BF16) for i in range(2)]
        t_qs_mm = {0: None, 1: None}
        for e_ in range(16):
            bf = e_ % 2
            t_wl = B.dma("sync", wqs[bf][:, :], wq_b[e_], f"dwq{bf}",
                         wait=[conv_tok["wq"], t_qs_mm[bf], t_lnx])
            for c in range(16):
                t_qs_mm[bf] = B.op("tensor", lambda e, e_=e_, c=c, bf=bf: e.matmul(
                    bank(4)[:, 16 + e_:17 + e_], wqs[bf][:, 128 * c:128 * (c + 1)],
                    lnx_fmb[:, c:c + 1], start=(c == 0), stop=(c == 15)),
                    wait=[t_wl, t_lnxfmb], inc="pe" if c == 15 else None)
        t_qsfm = B.op("vector", lambda e: e.tensor_copy(qsum_fmb[:, :], bank(4)[:, 16:32]),
                      wait=[t_qs_mm[0], t_qs_mm[1]], inc="ve")
        t_rel_mm = None
        for h in range(H):
            t_rel_mm = B.op("tensor", lambda e, h=h: e.matmul(
                bank(5)[0:4, 16 + h:17 + h], mnT_bf[:, 4 * h:4 * h + 4],
                qsum_fmb[:, h:h + 1], start=True, stop=True),
                wait=[t_qsfm, t_mnbf], inc="pe" if h == 15 else None)
        t_rel = B.op("vector", lambda e: e.tensor_scalar(
            rel_sb[:, :], bank(5)[0:4, 16:32], SCALE / S, None, ALU.mult),
            wait=[t_rel_mm], inc="ve")
        t_reld = B.dma("scalar", relf, rel_sb[:, :], "dbw", wait=[t_rel])
        t_relT = B.dma("sync", relT[:, :], relf.rearrange("m h -> h m"), "dbnc",
                       wait=[t_reld], slow=True)
        t_mx = B.op("vector", lambda e: e.reduce_max(small[0:16, 29:30], relT[:, :], axis=AX.X),
                    wait=[t_relT], inc="ve")
        t_nmx = B.op("vector", lambda e: e.tensor_scalar(
            small[0:16, 30:31], small[0:16, 29:30], -1.0, None, ALU.mult), wait=[t_mx], inc="ve")
        t_ex = B.op("scalar", lambda e: e.activation(
            w_sb[:, :], relT[:, :], AF.Exp, bias=small[0:16, 30:31]), wait=[t_nmx], inc="sc")
        t_sm = B.op("vector", lambda e: e.reduce_sum(small[0:16, 31:32], w_sb[:, :], axis=AX.X),
                    wait=[t_ex], inc="ve")
        t_rs = B.op("vector", lambda e: e.reciprocal(small[0:16, 32:33], small[0:16, 31:32]),
                    wait=[t_sm], inc="ve")
        t_g = B.op("scalar", lambda e: e.activation(small[0:16, 33:34], gate_sb[:, :], AF.Sigmoid),
                   wait=[t_gate, t_ex], inc="sc")
        t_w = B.op("vector", lambda e: e.tensor_scalar(
            w_sb[:, :], w_sb[:, :], small[0:16, 32:33], None, ALU.mult),
            wait=[t_rs], inc="ve")
        t_gw = B.op("vector", lambda e: e.tensor_scalar(
            w_sb[:, :], w_sb[:, :], small[0:16, 33:34], None, ALU.mult),
            wait=[t_w, t_g], inc="ve")
        t_omg = B.op("vector", lambda e: e.tensor_scalar(
            small[0:16, 34:35], small[0:16, 33:34], -1.0, 1.0, ALU.mult, ALU.add),
            wait=[t_g], inc="ve")
        t_gwd = B.dma("scalar", gwf, w_sb[:, :], "dbw", wait=[t_gw, t_reld])
        t_omgd = B.dma("scalar", omgf, small[0:16, 34:35], "dbw", wait=[t_omg, t_gwd])
        gw_row = p0sbuf("gw_row", [1, 64], F32)
        omg_row = p0sbuf("omg_row", [1, 16], F32)
        t_gwr = B.dma("sync", gw_row[:, :], gwf.rearrange("a b -> (a b)")[None, :],
                      "dbnc", wait=[t_gwd, t_relT])
        t_omgr = B.dma("sync", omg_row[:, :], omgf.rearrange("a b -> (a b)")[None, :],
                       "dbnc", wait=[t_omgd, t_gwr])
        t_gwbm = B.op("tensor", lambda e: e.matmul(
            bank(5)[:, 64:128], onesf_row[:, :], gw_row[:, :], start=True, stop=True),
            wait=[t_gwr, t_onesr], inc="pe")
        t_omgbm = B.op("tensor", lambda e: e.matmul(
            bank(5)[:, 128:144], onesf_row[:, :], omg_row[:, :], start=True, stop=True),
            wait=[t_omgr, t_onesr], inc="pe")
        t_gwc = B.op("vector", lambda e: e.tensor_copy(gwcol[:, :], bank(5)[:, 64:128]),
                     wait=[t_gwbm], inc="ve")
        t_omgc = B.op("vector", lambda e: e.tensor_copy(omgcol[:, :], bank(5)[:, 128:144]),
                      wait=[t_omgbm], inc="ve")
        if dump:
            B.dma("scalar", dbg_st[:, 0:32], stats_r[:, :], "dbg", wait=[t7], slow=True)
            B.dma("scalar", dbg_st[:, 32:64], stats_nmr[:, :], "dbg", wait=[t7], slow=True)
            B.dma("scalar", dbg_st[:, 64:80], lnx_fm[:, :], "dbg", wait=[t_lnxfm], slow=True)
            B.dma("scalar", dbg_st[0:16, 80:84], w_sb[:, :], "dbg", wait=[t_gw], slow=True)
            B.dma("scalar", dbg_st[:, 84:148], gwcol[:, :], "dbg", wait=[t_gwc], slow=True)
            B.dma("scalar", dbg_st[0:1, 148:149], small[0:1, 11:12], "dbg", wait=[t_rm3], slow=True)
        if dump:
            B.op("sync", lambda e: [e.wait_ge(B.handles["dbg"], B.ctr["dbg"])])
        B.flush("p0")
        p0stk.close()
        p0_drain = [B.tok("dbw"), B.tok("dbnc")] + ([B.tok("dbg")] if dump else [])
        if stop_after == "p0":
            return nc, B

        pastk = ExitStack()
        pasbuf = lambda n, s, d: pastk.enter_context(nc.sbuf_tensor(n, s, d))
        kvst = pasbuf("kvst", [128, H * 128], F32)
        kvst_b = pasbuf("kvst_b", [128, H * 128], BF16)
        zst = pasbuf("zst", [128, H], F32)
        zst_b = pasbuf("zst_b", [128, H], BF16)
        memsb = pasbuf("memsb", [128, H * 512], BF16)
        for h in range(H):
            B.dma("gpsimd", memsb[:, 512 * h:512 * (h + 1)],
                  mems_h[h].rearrange("p m e -> p (m e)"), "dmem", wait=p0_drain)
        t_mem = [B.tok("dmem")] * H
        conv_items = []
        for (nm, csrc, cdst, nch) in [("ew", ew_hh, ew_b, 16), ("down", down_hh, down_b, 64)]:
            step = 4 if nch == 16 else 8
            for a in range(0, nch, step):
                conv_items.append((nm, csrc[a:a + step], cdst[a:a + step]))
        xln = [pasbuf(f"xln{i}", [128, 2048], F32) for i in range(2)]
        xres = [pasbuf(f"xres{i}", [128, 2048], F32) for i in range(2)]
        ln1tm_s = pasbuf("ln1tm", [128, 2048], BF16)
        ln1tm = [ln1tm_s, ln1tm_s]
        ln1fm = pasbuf("ln1fm", [128, 8192], BF16)
        qp = pasbuf("qp", [128, 8192], BF16)
        kpt = pasbuf("kpt", [128, TPB * 2048], BF16)
        vct = pasbuf("vct", [128, TPB * 2048], BF16)
        kpf2 = [pasbuf(f"kpf{i}", [128, 2048], BF16) for i in range(2)]
        comb_t_s = pasbuf("comb_t", [128, 2048], BF16)
        comb_t = [comb_t_s, comb_t_s]
        elu_t_s = pasbuf("elu_t", [128, 512], BF16)
        elu_t = [elu_t_s, elu_t_s]
        elu_e = [pasbuf(f"elu_e{i}", [128, 512], BF16) for i in range(2)]
        x1blk = pasbuf("x1blk", [128, 8192], BF16)
        x1part = [pasbuf(f"x1part{i}", [128, 512], F32) for i in range(2)]
        ln2tm = pasbuf("ln2tm", [128, 2048], BF16)
        ln2fm = pasbuf("ln2fm", [128, 8192], BF16)
        wbufW = [pasbuf(f"wbufW{i}", [128, 2048], BF16) for i in range(3)]
        wbufQ = wbufW
        wbufS = wbufW
        sc_sb = [pasbuf(f"sc_sb{i}", [128, 128], BF16) for i in range(3)]
        lsc = [pasbuf(f"lsc{i}", [128, 128], BF16) for i in range(3)]
        rns_all = [pasbuf(f"rns_all{i}", [128, 64], F32) for i in range(2)]
        macc = [pasbuf(f"macc{i}", [128, 512], BF16) for i in range(3)]
        att_sm2 = [pasbuf(f"att_sm{i}", [128, 16], F32) for i in range(3)]

        state_rdy = {("kv", h): None for h in range(H)}
        state_rdy.update({("z", h): None for h in range(H)})
        state_rdy.update({("cast", j): None for j in range(4)})
        ln1fm_free = None
        ln2fm_free = None
        qkv_free = None
        t_mmQ_prev = {0: None, 1: None, 2: None}
        x1ev_hist = {}
        att_prev_p = {0: None, 1: None, 2: None, 3: None}
        x1s_tok = {}
        ln2s_tok = {}
        t_lnv_prev = t_lns_prev = None

        def emit_ln2(bb, x1ev):
            nonlocal t_lnv_prev, t_lns_prev, ln2fm_free
            t_ln2_last = None
            for t in range(TPB):
                T = TPB * bb + t
                x1_sl = x1blk[:, 2048 * t:2048 * (t + 1)]
                rdy = [x1ev[(t, 1, 0)], x1ev[(t, 1, 1)],
                       x1ev[(t, 0, 0)], x1ev[(t, 0, 1)]]
                t_s2 = B.op("vector", lambda e, x1_sl=x1_sl: e.reduce_sum(
                    small[:, 60:61], x1_sl, axis=AX.X), wait=rdy + [t_lnv_prev], inc="ve")
                t_q2 = B.op("scalar", lambda e, x1_sl=x1_sl: e.activation(
                    scr[:, :], x1_sl, AF.Square, accum_out=small[:, 61:62]),
                    wait=rdy + [t_lnv_prev], inc="sc")
                u1 = B.op("vector", lambda e: e.tensor_scalar(
                    small[:, 62:63], small[:, 60:61], 1.0 / D, None, ALU.mult),
                    wait=[t_s2], inc="ve")
                u2 = B.op("vector", lambda e: e.tensor_scalar(
                    small[:, 63:64], small[:, 61:62], 1.0 / D, EPS_LN, ALU.mult, ALU.add),
                    wait=[t_q2], inc="ve")
                u3 = B.op("vector", lambda e: e.tensor_tensor(
                    small[:, 64:65], small[:, 62:63], small[:, 62:63], ALU.mult),
                    wait=[u1], inc="ve")
                u4 = B.op("vector", lambda e: e.tensor_tensor(
                    small[:, 65:66], small[:, 63:64], small[:, 64:65], ALU.subtract),
                    wait=[u2, u3], inc="ve")
                u5 = B.op("scalar", lambda e: e.activation(
                    small[:, 66:67], small[:, 65:66], AF.Ln),
                    wait=[u4, t_lns_prev], inc="sc")
                u6 = B.op("scalar", lambda e: e.activation(
                    small[:, 67:68], small[:, 66:67], AF.Exp, scale=-0.5),
                    wait=[u5], inc="sc")
                u7 = B.op("vector", lambda e: e.scalar_tensor_tensor(
                    small[:, 68:69], small[:, 62:63], -1.0, small[:, 67:68],
                    ALU.mult, ALU.mult), wait=[u6], inc="ve")
                t_lnv_prev = u7
                t_ln2_h = {}
                for hf in range(2):
                    t_ln2_h[hf] = B.op("scalar", lambda e, x1_sl=x1_sl, hf=hf: e.activation(
                        ln2tm[:, 1024 * hf:1024 * (hf + 1)],
                        x1_sl[:, 1024 * hf:1024 * (hf + 1)], AF.Identity,
                        scale=small[:, 67:68], bias=small[:, 68:69]),
                        wait=[u7, buf_free.get(("ln2tm", hf))], inc="sc")
                t_lns_prev = t_ln2_h[1]
                buf_free[("x1blk", t)] = t_ln2_h[1]
                for g in range(4):
                    bk = 3 if g % 2 == 0 else 7
                    t_trm = None
                    for cc in range(4):
                            c = 4 * g + cc
                            t_trm = B.op("tensor", lambda e, c=c, cc=cc, bk=bk, g=g: e.matmul(
                                bank(bk)[:, 128 * cc:128 * (cc + 1)],
                                ln2tm[:, 128 * c:128 * (c + 1)], identb[:, :],
                                start=True, stop=True),
                                wait=[t_ln2_h[g // 2], bank_free[bk], ln2fm_free],
                                inc="pe" if cc == 3 else None)
                    dst = ln2fm.ap().rearrange("p (c s) -> p c s", s=512)[
                            :, 4 * g:4 * g + 4, 128 * t:128 * (t + 1)]
                    tev = B.op("vector", lambda e, dst=dst, bk=bk: e.tensor_copy(
                            dst, bank(bk).rearrange("p (c s) -> p c s", s=128)),
                            wait=[t_trm], inc="ve")
                    bank_free[bk] = tev
                    if g == 1:
                            buf_free[("ln2tm", 0)] = tev
                    if g == 3:
                            buf_free[("ln2tm", 1)] = tev
                            t_ln2_last = tev
            t_ld = None
            for c in range(16):
                t_ld = B.dma("scalar", ln2s[c][:, 512 * bb:512 * (bb + 1)],
                                 ln2fm[:, 512 * c:512 * (c + 1)], "dl2w", wait=[t_ln2_last])
            ln2fm_free = t_ld
            ln2s_tok[bb] = t_ld

        for b in range(n_blocks):
            if b >= 1:
                take = conv_items[:2] if b < n_blocks - 1 else conv_items
                conv_items = conv_items[len(take):]
                for (nm, csrc, cdst) in take:
                    B.dma("gpsimd", cdst, csrc, "dcv_" + nm)
                    conv_tok[nm] = B.tok("dcv_" + nm)
            t_ln1_all = None
            for t in range(TPB):
                T = TPB * b + t
                xk = B.dma("sync", xln[t % 2][:, :], xt[T], f"dx{t % 2}",
                           wait=[buf_free.get(("xln", t % 2))] + p0_drain)
                t_ln = B.op("scalar", lambda e, t=t, T=T: e.activation(
                    ln1tm[t % 2][:, :], xln[t % 2][:, :], AF.Identity,
                    scale=stats_r[:, T:T + 1], bias=stats_nmr[:, T:T + 1]),
                    wait=[xk, buf_free.get("ln1tm")], inc="sc")
                buf_free[("xln", t % 2)] = t_ln
                for g in range(4):
                    bk = 3 if g % 2 == 0 else 7
                    t_trm = None
                    for cc in range(4):
                        c = 4 * g + cc
                        t_trm = B.op("tensor", lambda e, t=t, c=c, bk=bk, cc=cc: e.matmul(
                            bank(bk)[:, 128 * cc:128 * (cc + 1)],
                            ln1tm[t % 2][:, 128 * c:128 * (c + 1)], identb[:, :],
                            start=True, stop=True),
                            wait=[t_ln, t_id, bank_free[bk], ln1fm_free.get(c)
                                  if isinstance(ln1fm_free, dict) else ln1fm_free],
                            inc="pe" if cc == 3 else None)
                    dst = ln1fm.ap().rearrange("p (c s) -> p c s", s=512)[
                        :, 4 * g:4 * g + 4, 128 * t:128 * (t + 1)]
                    tev = B.op("vector", lambda e, dst=dst, bk=bk: e.tensor_copy(
                        dst, bank(bk).rearrange("p (c s) -> p c s", s=128)),
                        wait=[t_trm], inc="ve")
                    bank_free[bk] = tev
                    t_ln1_all = tev
                buf_free["ln1tm"] = t_ln1_all
            if b > 0:
                emit_ln2(b - 1, x1ev_hist[b - 1])

            t_qp_all = None
            for e_ in range(16):
                bf = e_ % 3
                t_wl = B.dma("sync", wbufQ[bf][:, :], wq_b[e_], f"dws{bf}",
                             wait=[conv_tok["wq"], t_mmQ_prev[bf],
                                   buf_free.get(("wbufS", bf))] + p0_drain)
                bk = 2 + bf
                t_mm = None
                for c in range(16):
                    t_mm = B.op("tensor", lambda e, c=c, bk=bk, bf=bf: e.matmul(
                        bank(bk), wbufQ[bf][:, 128 * c:128 * (c + 1)],
                        ln1fm[:, 512 * c:512 * (c + 1)],
                        start=(c == 0), stop=(c == 15)),
                        wait=[t_wl, t_ln1_all, bank_free[bk] if c == 0 else None],
                        inc="pe" if c == 15 else None)
                t_mmQ_prev[bf] = t_mm
                buf_free[("wbufS", bf)] = t_mm
                ebf = e_ % 2
                t_t = B.op("scalar", lambda e, bk=bk, ebf=ebf: e.activation(
                    elu_t[ebf][:, :], bank(bk), AF.Relu, scale=-1.0),
                    wait=[t_mm, buf_free.get(("elu", ebf)), buf_free.get("elu_t")], inc="sc")
                t_e = B.op("scalar", lambda e, ebf=ebf: e.activation(
                    elu_e[ebf][:, :], elu_t[ebf][:, :], AF.Exp, scale=-1.0),
                    wait=[t_t], inc="sc")
                buf_free["elu_t"] = t_e
                t_qp = B.op("vector", lambda e, e_=e_, bk=bk, ebf=ebf: e.scalar_tensor_tensor(
                    qp[:, 512 * e_:512 * (e_ + 1)], bank(bk), 0.0, elu_e[ebf][:, :],
                    ALU.max, ALU.add), wait=[t_e, qkv_free], inc="ve")
                bank_free[bk] = t_qp
                buf_free[("elu", ebf)] = t_qp
                t_qp_all = t_qp

            def tm_proj(wsrc_ap, wsrc_g_ap, wtok, dst_buf, elu):
                ev_last = None
                wsrc_g = wsrc_g_ap
                for eh in range(2):
                    group_last = {}
                    for cp in range(8):
                        bf = cp % 3
                        t_wl = B.dma("sync", wbufS[bf].ap().rearrange("p (a e) -> p a e", e=1024),
                                     wsrc_g[cp][:, :, 1024 * eh:1024 * (eh + 1)],
                                     f"dws{bf}",
                                     wait=[wtok, buf_free.get(("wbufS", bf))] + p0_drain)
                        tm = None
                        for a_ in range(2):
                            c = 2 * cp + a_
                            for t in range(TPB):
                                for n_ in range(2):
                                    bk = 2 * t + n_
                                    is_last_of_c = (t == TPB - 1 and n_ == 1 and a_ == 1)
                                    tm = B.op("tensor", lambda e, a_=a_, t=t, n_=n_, c=c, bk=bk, bf=bf: e.matmul(
                                        bank(bk), ln1fm[:, 512 * c + 128 * t:512 * c + 128 * (t + 1)],
                                        wbufS[bf][:, 1024 * a_ + 512 * n_:1024 * a_ + 512 * (n_ + 1)],
                                        start=(c == 0), stop=(c == 15)),
                                        wait=[t_wl, t_ln1_all,
                                              bank_free[bk] if c == 0 else None],
                                        inc="pe" if (is_last_of_c or c == 15) else None)
                                    if c == 15:
                                        group_last[(t, n_)] = tm
                        buf_free[("wbufS", bf)] = tm
                    for t in range(TPB):
                        for n_ in range(2):
                            bk = 2 * t + n_
                            off = 2048 * t + 1024 * eh + 512 * n_
                            t_mm = group_last[(TPB - 1, 1)]
                            if elu:
                                bf2 = (2 * t + n_) % 2
                                t_t = B.op("scalar", lambda e, bk=bk, bf2=bf2: e.activation(
                                    elu_t[bf2][:, :], bank(bk), AF.Relu, scale=-1.0),
                                    wait=[t_mm, buf_free.get(("elu", bf2)), buf_free.get("elu_t")], inc="sc")
                                t_e = B.op("scalar", lambda e, bf2=bf2: e.activation(
                                    elu_e[bf2][:, :], elu_t[bf2][:, :], AF.Exp, scale=-1.0),
                                    wait=[t_t], inc="sc")
                                buf_free["elu_t"] = t_e
                                tev = B.op("vector", lambda e, off=off, bk=bk, bf2=bf2: e.scalar_tensor_tensor(
                                    dst_buf[:, off:off + 512], bank(bk), 0.0, elu_e[bf2][:, :],
                                    ALU.max, ALU.add), wait=[t_e, qkv_free], inc="ve")
                                buf_free[("elu", bf2)] = tev
                            else:
                                tev = B.op("vector", lambda e, off=off, bk=bk: e.tensor_copy(
                                    dst_buf[:, off:off + 512], bank(bk)),
                                    wait=[t_mm, qkv_free], inc="ve")
                            bank_free[bk] = tev
                            ev_last = tev
                return ev_last

            wk_g = wk_b.rearrange("(g a) p e -> g p a e", a=2)
            wv_g = wv_b.rearrange("(g a) p e -> g p a e", a=2)
            t_kpt_all = tm_proj(None, wk_g, conv_tok["wk"], kpt, elu=True)
            t_vct_all = tm_proj(None, wv_g, conv_tok["wv"], vct, elu=False)

            comb_fm_rdy = None
            last_c8 = None
            for t in range(TPB):
                T = TPB * b + t
                kpf = kpf2[t % 2]
                kpf_ev = None
                for g in range(4):
                    bk = 3 if g % 2 == 0 else 7
                    t_trm = None
                    for hh in range(4):
                        h = 4 * g + hh
                        t_trm = B.op("tensor", lambda e, t=t, h=h, hh=hh, bk=bk: e.matmul(
                            bank(bk)[:, 128 * hh:128 * (hh + 1)],
                            kpt[:, 2048 * t + 128 * h:2048 * t + 128 * (h + 1)], identb[:, :],
                            start=True, stop=True),
                            wait=[t_kpt_all, bank_free[bk],
                                  buf_free.get(("kpf", t % 2)) if (g == 0 and hh == 0) else None],
                            inc="pe" if hh == 3 else None)
                    tev = B.op("vector", lambda e, kpf=kpf, g=g, bk=bk: e.tensor_copy(
                        kpf[:, 512 * g:512 * (g + 1)], bank(bk)),
                        wait=[t_trm], inc="ve")
                    bank_free[bk] = tev
                    kpf_ev = tev
                t_kpf = kpf_ev

                first = (T == 0)
                pre_ev = {}
                for h in range(H):
                    bS = 4 + h % 3
                    kpt_sl = kpt[:, 2048 * t + 128 * h:2048 * t + 128 * (h + 1)]
                    vct_sl = vct[:, 2048 * t + 128 * h:2048 * t + 128 * (h + 1)]
                    t_kv = B.op("tensor", lambda e, kpt_sl=kpt_sl, vct_sl=vct_sl, bS=bS: e.matmul(
                        bank(bS)[:, 0:128], kpt_sl, vct_sl, start=True, stop=True),
                        wait=[t_kpt_all, t_vct_all, bank_free[bS],
                              state_rdy[("cast", h // 4)]], inc=None)
                    t_z = B.op("tensor", lambda e, kpt_sl=kpt_sl, bS=bS: e.matmul(
                        bank(bS)[:, 132:133], kpt_sl, ones_bf[:, :], start=True, stop=True),
                        wait=[t_ones], inc="pe")
                    if first:
                        t_kvu = B.op("vector", lambda e, h=h, bS=bS: e.tensor_copy(
                            kvst[:, 128 * h:128 * (h + 1)], bank(bS)[:, 0:128]),
                            wait=[t_z], inc="ve")
                        t_zu = B.op("vector", lambda e, h=h, bS=bS: e.tensor_copy(
                            zst[:, h:h + 1], bank(bS)[:, 132:133]),
                            wait=[t_z], inc="ve")
                    else:
                        t_kvu = B.op("vector", lambda e, h=h, bS=bS: e.tensor_tensor(
                            kvst[:, 128 * h:128 * (h + 1)], kvst[:, 128 * h:128 * (h + 1)],
                            bank(bS)[:, 0:128], ALU.add), wait=[t_z], inc="ve")
                        t_zu = B.op("vector", lambda e, h=h, bS=bS: e.tensor_tensor(
                            zst[:, h:h + 1], zst[:, h:h + 1], bank(bS)[:, 132:133], ALU.add),
                            wait=[t_z], inc="ve")
                    bank_free[bS] = t_zu
                    pre_ev[h] = (t_kvu, t_zu)

                t_nmm = None
                for h in range(H):
                    t_nmm = B.op("tensor", lambda e, h=h, t=t: e.matmul(
                        bank(3)[:, 4 * h:4 * h + 4],
                        qp[:, 512 * h + 128 * t: 512 * h + 128 * (t + 1)],
                        mnT_bf[:, 4 * h:4 * h + 4], start=True, stop=True),
                        wait=[t_qp_all, t_mnbf, bank_free[3],
                              buf_free.get(("rns", t % 2))],
                        inc="pe" if h == H - 1 else None)
                ra = rns_all[t % 2]
                r1 = B.op("vector", lambda e, ra=ra: e.tensor_scalar(
                    ra[:, :], bank(3)[:, 0:64], EPS_ATT, None, ALU.max),
                    wait=[t_nmm, buf_free.get(("rns", t % 2))], inc="ve")
                bank_free[3] = r1
                r2 = B.op("vector", lambda e, ra=ra: e.reciprocal(ra[:, :], ra[:, :]),
                          wait=[r1], inc="ve")
                t_rns = B.op("vector", lambda e, ra=ra: e.tensor_tensor(
                    ra[:, :], ra[:, :], gwcol[:, :], ALU.mult),
                    wait=[r2, t_gwc], inc="ve")

                grp1 = {}
                for hx in range(H + 2):
                    if hx < H:
                        h = hx
                        si = h % 3
                        bA, bM = si, 4 + si
                        qp_sl = qp[:, 512 * h + 128 * t: 512 * h + 128 * (t + 1)]
                        t_m1 = B.op("tensor", lambda e, qp_sl=qp_sl, h=h, bA=bA: e.matmul(
                            bank(bA), qp_sl, memsb[:, 512 * h:512 * (h + 1)],
                            start=True, stop=True),
                            wait=[t_qp_all, t_mem[h], bank_free[bA]], inc="pe")
                        t_m3 = B.op("tensor", lambda e, kpf=kpf, qp_sl=qp_sl, h=h, bM=bM: e.matmul(
                            bank(bM)[:, 0:128], kpf[:, 128 * h:128 * (h + 1)], qp_sl,
                            start=True, stop=True),
                            wait=[t_kpf, bank_free[bM]], inc="pe")
                        t_scev = B.op("vector", lambda e, h=h, bM=bM: e.tensor_tensor(
                            sc_sb[h % 3][:, :], bank(bM)[:, 0:128], maskT[:, :], ALU.mult),
                            wait=[t_m3, t_mask, buf_free.get(("sc", h % 3))], inc="ve")
                        grp1[h] = (t_m1, t_scev, bA, bM, qp_sl)
                    if hx >= 2:
                        h = hx - 2
                        (t_m1, t_scev, bA, bM, qp_sl) = grp1[h]
                        if not first:
                            B.op("tensor", lambda e, qp_sl=qp_sl, h=h, bM=bM: e.matmul(
                                bank(bM)[:, 128:256], qp_sl, kvst_b[:, 128 * h:128 * (h + 1)],
                                start=True, stop=False),
                                wait=[state_rdy[("kv", h)], t_scev])
                        B.op("tensor", lambda e, h=h, bM=bM, first=first, t=t: e.matmul(
                            bank(bM)[:, 128:256], sc_sb[h % 3][:, :],
                            vct[:, 2048 * t + 128 * h:2048 * t + 128 * (h + 1)],
                            start=first, stop=True), wait=[t_scev, t_vct_all])
                        if not first:
                            B.op("tensor", lambda e, qp_sl=qp_sl, h=h, bM=bM: e.matmul(
                                bank(bM)[:, 264:265], qp_sl, zst_b[:, h:h + 1],
                                start=True, stop=False),
                                wait=[state_rdy[("z", h)]])
                        t_den = B.op("tensor", lambda e, h=h, bM=bM, first=first: e.matmul(
                            bank(bM)[:, 264:265], sc_sb[h % 3][:, :], ones_bf[:, :],
                            start=first, stop=True), wait=[t_ones], inc="pe")
                        asm = att_sm2[h % 3]
                        d1 = B.op("vector", lambda e, asm=asm, bM=bM: e.tensor_scalar(
                            asm[:, 8:9], bank(bM)[:, 264:265], EPS_ATT, None,
                            ALU.max), wait=[t_den] + list(att_prev_p[h % 3] or ()), inc="ve")
                        d2 = B.op("vector", lambda e, asm=asm: e.reciprocal(asm[:, 8:9], asm[:, 8:9]),
                                  wait=[d1], inc="ve")
                        c6 = B.op("vector", lambda e, asm=asm, h=h: e.tensor_tensor(
                            asm[:, 8:9], asm[:, 8:9], omgcol[:, h:h + 1], ALU.mult),
                            wait=[d2, t_omgc], inc="ve")
                        mc = macc[h % 3]
                        c7s = []
                        for m_ in range(4):
                            c7s.append(B.op("scalar", lambda e, ra=ra, mc=mc, bA=bA, h=h, m_=m_: e.activation(
                                mc[:, 128 * m_:128 * (m_ + 1)], bank(bA)[:, 128 * m_:128 * (m_ + 1)],
                                AF.Identity, scale=ra[:, 4 * h + m_:4 * h + m_ + 1]),
                                wait=[t_m1, t_rns, buf_free.get(("acc", h % 3))], inc="sc"))
                        bank_free[bA] = c7s[-1]
                        c9 = B.op("scalar", lambda e, asm=asm, h=h, bM=bM: e.activation(
                            lsc[h % 3][:, :], bank(bM)[:, 128:256], AF.Identity,
                            scale=asm[:, 8:9]),
                            wait=[c6, buf_free.get(("lsc", h % 3))], inc="sc")
                        bank_free[bM] = c9
                        a1 = B.op("vector", lambda e, mc=mc: e.tensor_tensor(
                            mc[:, 0:128], mc[:, 0:128], mc[:, 128:256], ALU.add),
                            wait=[c7s[1]], inc="ve")
                        a2 = B.op("vector", lambda e, mc=mc: e.tensor_tensor(
                            mc[:, 256:384], mc[:, 256:384], mc[:, 384:512], ALU.add),
                            wait=[c7s[3]], inc="ve")
                        a3 = B.op("gpsimd", lambda e, mc=mc: e.tensor_tensor(
                            mc[:, 0:128], mc[:, 0:128], mc[:, 256:384], ALU.add),
                            wait=[a1, a2], inc="po")
                        c10 = B.op("gpsimd", lambda e, t=t, h=h, mc=mc: e.tensor_tensor(
                            comb_t[t % 2][:, 128 * h:128 * (h + 1)], lsc[h % 3][:, :],
                            mc[:, 0:128], ALU.add),
                            wait=[a3, c9, buf_free.get("comb") if h == 0 else None],
                            inc="po")
                        buf_free[("lsc", h % 3)] = c10
                        buf_free[("acc", h % 3)] = c10
                        att_prev_p[h % 3] = (c9,)
                        buf_free[("sc", h % 3)] = t_den
                        last_c8 = c10
                cast_w = [t_den] + [pre_ev[h][0] for h in range(H)] + [pre_ev[15][1]]
                for j in range(4):
                    state_rdy[("cast", j)] = B.op("vector", lambda e, j=j: e.tensor_copy(
                        kvst_b[:, 512 * j:512 * (j + 1)], kvst[:, 512 * j:512 * (j + 1)]),
                        wait=cast_w, inc="ve")
                    for hh in range(4 * j, 4 * j + 4):
                        state_rdy[("kv", hh)] = state_rdy[("cast", j)]
                t_zc = B.op("vector", lambda e: e.tensor_copy(zst_b[:, :], zst[:, :]),
                            wait=cast_w, inc="ve")
                for hh in range(H):
                    state_rdy[("z", hh)] = t_zc
                buf_free[("kpf", t % 2)] = last_c8
                buf_free[("rns", t % 2)] = c7s[-1]
                for g in range(4):
                    bk = 3 if g % 2 == 0 else 7
                    t_trm = None
                    for hh in range(4):
                        h = 4 * g + hh
                        t_trm = B.op("tensor", lambda e, t=t, h=h, hh=hh, bk=bk: e.matmul(
                            bank(bk)[:, 128 * hh:128 * (hh + 1)],
                            comb_t[t % 2][:, 128 * h:128 * (h + 1)], identb[:, :],
                            start=True, stop=True),
                            wait=[last_c8, bank_free[bk], t_vct_all],
                            inc="pe" if hh == 3 else None)
                    dst = ln1fm.ap().rearrange("p (c s) -> p c s", s=512)[
                        :, 4 * g:4 * g + 4, 128 * t:128 * (t + 1)]
                    tev = B.op("vector", lambda e, dst=dst, bk=bk: e.tensor_copy(
                        dst, bank(bk).rearrange("p (c s) -> p c s", s=128)),
                        wait=[t_trm], inc="ve")
                    bank_free[bk] = tev
                    if g == 3:
                        buf_free["comb"] = tev
                        comb_fm_rdy = tev
            qkv_free = last_c8

            if dump and b == 0:
                B.dma("scalar", dbg_qp[:, :], qp[:, :], "dbg", wait=[t_qp_all])
                B.dma("scalar", dbg_kpt[:, :], kpt[:, :], "dbg", wait=[t_kpt_all])
                B.dma("scalar", dbg_vct[:, :], vct[:, :], "dbg", wait=[t_vct_all])
                B.dma("scalar", dbg_comb[:, :], ln1fm[:, :], "dbg", wait=[comb_fm_rdy])

            t_o_mm_last = None
            x1blk_ev = {}
            o_cfree = {}
            for eh in range(2):
                group_last = {}
                wo_g = wo_b.rearrange("(g a) p e -> g p a e", a=2)
                for cp in range(8):
                    bf = cp % 3
                    t_wl = B.dma("sync", wbufS[bf].ap().rearrange("p (a e) -> p a e", e=1024),
                                 wo_g[cp][:, :, 1024 * eh:1024 * (eh + 1)],
                                 f"dws{bf}",
                                 wait=[conv_tok["wo"], buf_free.get(("wbufS", bf))])
                    tm = None
                    for a_ in range(2):
                        c = 2 * cp + a_
                        for t in range(TPB):
                            for n_ in range(2):
                                bk = 2 * t + n_
                                is_last = (t == TPB - 1 and n_ == 1 and a_ == 1)
                                is_c_last = (t == TPB - 1 and n_ == 1)
                                tm = B.op("tensor", lambda e, a_=a_, t=t, n_=n_, c=c, bk=bk, bf=bf: e.matmul(
                                    bank(bk), ln1fm[:, 512 * c + 128 * t:512 * c + 128 * (t + 1)],
                                    wbufS[bf][:, 1024 * a_ + 512 * n_:1024 * a_ + 512 * (n_ + 1)],
                                    start=(c == 0), stop=(c == 15)),
                                    wait=[t_wl, comb_fm_rdy, bank_free[bk] if c == 0 else None],
                                    inc="pe" if (is_last or c == 15 or (eh == 1 and is_c_last)) else None)
                                if c == 15:
                                    group_last[(t, n_)] = tm
                                if eh == 1 and is_c_last:
                                    o_cfree[c] = tm
                    buf_free[("wbufS", bf)] = tm
                t_o_mm_last = group_last[(TPB - 1, 1)]
                for t in range(TPB):
                    T = TPB * b + t
                    xrk = B.dma("sync", xres[t % 2][:, :], xt[T], f"dxr{t % 2}",
                                wait=[buf_free.get(("xres", t % 2))] + p0_drain)
                    for n_ in range(2):
                        bk = 2 * t + n_
                        off = 1024 * eh + 512 * n_
                        pi = n_
                        tevf = B.op("vector", lambda e, pi=pi, t=t, off=off, bk=bk: e.scalar_tensor_tensor(
                            x1part[pi][:, :], bank(bk), 1.0,
                            xres[t % 2][:, off:off + 512], ALU.mult, ALU.add),
                            wait=[t_o_mm_last, xrk, buf_free.get(("x1p", pi))], inc="ve")
                        tevb = B.op("vector", lambda e, pi=pi, t=t, off=off: e.tensor_copy(
                            x1blk[:, 2048 * t + off:2048 * t + off + 512],
                            x1part[pi][:, :]),
                            wait=[tevf, buf_free.get(("x1blk", t))], inc="ve")
                        bank_free[bk] = tevf
                        x1blk_ev[(t, eh, n_)] = tevb
                        t_x1d = B.dma("scalar", x1t[T][:, off:off + 512],
                                      x1part[pi][:, :], f"dx1w{pi}", wait=[tevb])
                        buf_free[("x1p", pi)] = t_x1d
                        x1s_tok[(T, eh, n_)] = t_x1d
                    buf_free[("xres", t % 2)] = x1blk_ev[(t, eh, 1)]
            x1ev_hist[b] = dict(x1blk_ev)
            ln1fm_free = dict(o_cfree)

        emit_ln2(n_blocks - 1, x1ev_hist[n_blocks - 1])
        x1s_fin = [B.tok("dx1w0"), B.tok("dx1w1")]
        if dump:
            B.op("sync", lambda e: [e.wait_ge(B.handles[s], B.ctr[s]) for s in
                                    ["dbg", "dx1w0", "dx1w1", "dl2w"]])
        B.flush("pa")
        pastk.close()
        pa_drain = x1s_fin + [B.tok("dl2w")] + ([B.tok("dbg")] if dump else [])
        if stop_after == "pa":
            return nc, B

        pb1stk = ExitStack()
        pb1sbuf = lambda n, s, d: pb1stk.enter_context(nc.sbuf_tensor(n, s, d))
        ln2r = pb1sbuf("ln2r", [128, 65536], BF16)
        stage = [pb1sbuf(f"stage{i}", [128, 2048], BF16) for i in range(4)]
        upbuf = [pb1sbuf(f"upbuf{i}", [128, 2048], BF16) for i in range(2)]
        ustage = [pb1sbuf(f"ustage{i}", [128, 512], BF16) for i in range(2)]
        moeblk = pb1sbuf("moeblk", [128, 8192], BF16)
        wbufE = [pb1sbuf(f"wbufE{i}", [128, 2048], BF16) for i in range(2)]

        for c in range(16):
            B.dma("sync", ln2r[:, 4096 * c:4096 * (c + 1)], ln2s[c], "dl2r",
                  wait=[ln2s_tok[n_blocks - 1]] + pa_drain)
        t_l2r_all = B.tok("dl2r")

        t_top2 = {}
        for T in range(NT):
            bk = T % 2
            t_mm = None
            for c in range(16):
                t_mm = B.op("tensor", lambda e, T=T, c=c, bk=bk: e.matmul(
                    bank(bk)[:, 0:8], ln2r[:, 4096 * c + 128 * T:4096 * c + 128 * (T + 1)],
                    gatew_b[:, 8 * c:8 * (c + 1)], start=(c == 0), stop=(c == 15)),
                    wait=[t_l2r_all, t_gwb, bank_free[bk] if c == 0 else None],
                    inc="pe" if c == 15 else None)
            g1 = B.op("vector", lambda e, bk=bk: e.reduce_max(
                small[:, 70:71], bank(bk)[:, 0:8], axis=AX.X),
                wait=[t_mm, t_top2.get(T - 1)], inc="ve")
            g2 = B.op("vector", lambda e, bk=bk: e.tensor_scalar(
                small[:, 71:79], bank(bk)[:, 0:8], small[:, 70:71], None, ALU.is_ge),
                wait=[g1], inc="ve")
            g3 = B.op("vector", lambda e, bk=bk: e.scalar_tensor_tensor(
                small[:, 71:79], small[:, 71:79], -1e30, bank(bk)[:, 0:8],
                ALU.mult, ALU.add), wait=[g2], inc="ve")
            bank_free[bk] = g3
            g4 = B.op("vector", lambda e: e.reduce_max(
                small[:, 79:80], small[:, 71:79], axis=AX.X), wait=[g3], inc="ve")
            t_top2[T] = B.op("vector", lambda e, T=T: e.tensor_tensor(
                top2[:, T:T + 1], small[:, 70:71], small[:, 79:80], ALU.add),
                wait=[g4], inc="ve")

        moes_tok = {}
        for b in range(NB):
            ev_done = {}
            ew_g = ew_b.rearrange("(g a) p e -> g p a e", a=2)
            for eh in range(2):
                group_last = {}
                for cp in range(8):
                    bf = cp % 2
                    t_wl = B.dma("sync", wbufE[bf].ap().rearrange("p (a e) -> p a e", e=1024),
                                 ew_g[cp][:, :, 1024 * eh:1024 * (eh + 1)],
                                 f"dwe{bf}",
                                 wait=[conv_tok["ew"], buf_free.get(("wbufE", bf))] + pa_drain)
                    tm = None
                    for a_ in range(2):
                        c = 2 * cp + a_
                        for t in range(TPB):
                            T = TPB * b + t
                            for n_ in range(2):
                                bk = 2 * t + n_
                                is_last = (t == TPB - 1 and n_ == 1 and a_ == 1)
                                tm = B.op("tensor", lambda e, a_=a_, T=T, n_=n_, c=c, bk=bk, bf=bf: e.matmul(
                                    bank(bk), ln2r[:, 4096 * c + 128 * T:4096 * c + 128 * (T + 1)],
                                    wbufE[bf][:, 1024 * a_ + 512 * n_:1024 * a_ + 512 * (n_ + 1)],
                                    start=(c == 0), stop=(c == 15)),
                                    wait=[t_wl, t_l2r_all, bank_free[bk] if c == 0 else None],
                                    inc="pe" if (is_last or c == 15) else None)
                                if c == 15:
                                    group_last[(t, n_)] = tm
                    buf_free[("wbufE", bf)] = tm
                t_last = group_last[(TPB - 1, 1)]
                for t in range(TPB):
                    T = TPB * b + t
                    for n_ in range(2):
                        bk = 2 * t + n_
                        off = 2048 * t + 1024 * eh + 512 * n_
                        tev = B.op("vector", lambda e, T=T, off=off, bk=bk: e.tensor_scalar(
                            moeblk[:, off:off + 512], bank(bk), top2[:, T:T + 1], None,
                            ALU.mult), wait=[t_last, t_top2[T],
                                             buf_free.get("moeblk")], inc="ve")
                        bank_free[bk] = tev
                        ev_done[(t, eh, n_)] = tev
            t_md = None
            for t in range(TPB):
                T = TPB * b + t
                t_md = B.dma("scalar", moet[T], moeblk[:, 2048 * t:2048 * (t + 1)],
                             "dmw", wait=[ev_done[(t, 0, 0)], ev_done[(t, 0, 1)],
                                          ev_done[(t, 1, 0)], ev_done[(t, 1, 1)]])
                moes_tok[T] = t_md
            buf_free["moeblk"] = t_md

        hs_tok = {b: [] for b in range(NB)}
        relu_done = {}
        stage_dma = {}
        for half in range(2):
            for i in range(NI):
                pbf = i % 2
                t_ul = B.dma("gpsimd", upbuf[pbf][:, :], up_h[i], f"dup{pbf}",
                             wait=[buf_free.get(("upbuf", pbf))] + pa_drain)
                grp = {}
                for c in range(16):
                    for sb4 in range(4):
                        bk = 4 * pbf + sb4
                        blkid = 4 * half + sb4
                        tm = B.op("tensor", lambda e, i=i, c=c, sb4=sb4, bk=bk, pbf=pbf, blkid=blkid: e.matmul(
                            bank(bk), upbuf[pbf][:, 128 * c:128 * (c + 1)],
                            ln2r.ap().rearrange("p (c s) -> p c s", s=4096)[
                                :, c, 512 * blkid:512 * (blkid + 1)],
                            start=(c == 0), stop=(c == 15)),
                            wait=[t_ul, t_l2r_all,
                                  (relu_done.get((pbf, sb4)) or bank_free[bk]) if c == 0 else None],
                            inc="pe" if c == 15 else None)
                        if c == 15:
                            grp[sb4] = tm
                buf_free[("upbuf", pbf)] = grp[3]
                for sb4 in range(4):
                    bk = 4 * pbf + sb4
                    blkid = 4 * half + sb4
                    ubf = sb4 % 2
                    t_r = B.op("scalar", lambda e, bk=bk, ubf=ubf: e.activation(
                        ustage[ubf][:, :], bank(bk), AF.Relu),
                        wait=[grp[3], buf_free.get(("ustage", ubf))], inc="sc")
                    relu_done[(pbf, sb4)] = t_r
                    t_sqr = B.op("vector", lambda e, sb4=sb4, i=i, ubf=ubf: e.tensor_tensor(
                        stage[sb4][:, 512 * (i % 4):512 * (i % 4 + 1)],
                        ustage[ubf][:, :], ustage[ubf][:, :], ALU.mult),
                        wait=[t_r, stage_dma.get(sb4) if i % 4 == 0 else None], inc="ve")
                    buf_free[("ustage", ubf)] = t_sqr
                    if i % 4 == 3:
                        ig = i // 4
                        t_hd = B.dma("scalar", hs[blkid][:, ig * 4:ig * 4 + 4, :],
                                     stage[sb4].ap().rearrange("p (a s) -> p a s", s=512),
                                     f"dhs{sb4}", wait=[t_sqr])
                        stage_dma[sb4] = t_hd
                        hs_tok[blkid].append(t_hd)

        hs_fin = [B.tok(f"dhs{i}") for i in range(4)]
        moes_fin = B.tok("dmw")
        if dump:
            B.op("sync", lambda e: [e.wait_ge(B.handles[s], B.ctr[s]) for s in
                                    ["dmw", "dhs0", "dhs1", "dhs2", "dhs3"]])
        B.flush("pb1")
        pb1stk.close()
        pb1_drain = hs_fin + [moes_fin] + pa_drain
        if stop_after == "pb1":
            return nc, B

        pb2stk = ExitStack()
        pb2sbuf = lambda n, s, d: pb2stk.enter_context(nc.sbuf_tensor(n, s, d))
        hblk = pb2sbuf("hblk", [128, NI * 512], BF16)
        dbuf = [pb2sbuf(f"dbuf{i}", [128, 4096], BF16) for i in range(2)]
        x1tl = [pb2sbuf(f"x1tl{i}", [128, 2048], F32) for i in range(2)]
        moeh = pb2sbuf("moeh", [128, 4096], BF16)
        ybuf2 = [pb2sbuf(f"ybuf{i}", [128, 8192], BF16) for i in range(2)]
        ynb = pb2sbuf("ynb", [128, 2048], F32)
        gamma_bc = pb2sbuf("gamma_bc", [128, 2048], F32)
        beta_bc = pb2sbuf("beta_bc", [128, 2048], F32)
        xbb = pb2sbuf("xbb", [128, 2048], F32)
        outsb = [pb2sbuf(f"outsb{i}", [128, 2048], F32) for i in range(2)]
        gb_rows = pb2sbuf("gb_rows", [128, 2048], F32)
        t_grow = B.dma("sync", gb_rows[0:1, :], gamma_h, "dcst", wait=pb1_drain)

        gb_toks = []
        g_mms = []
        for e_ in range(4):
            tmg = B.op("tensor", lambda e, e_=e_: e.matmul(
                bank(e_), onesf_row[:, :], gb_rows[0:1, 512 * e_:512 * (e_ + 1)],
                start=True, stop=True), wait=[t_grow, t_onesr, bank_free[e_]], inc="pe")
            g_mms.append(tmg)
            tvg = B.op("vector", lambda e, e_=e_: e.tensor_copy(
                gamma_bc[:, 512 * e_:512 * (e_ + 1)], bank(e_)), wait=[tmg] + pb1_drain, inc="ve")
            gb_toks.append(tvg)
        t_brow = B.dma("sync", gb_rows[0:1, :], beta_h, "dcst", wait=g_mms)
        for e_ in range(4):
            tmb = B.op("tensor", lambda e, e_=e_: e.matmul(
                bank(4 + e_), onesf_row[:, :], gb_rows[0:1, 512 * e_:512 * (e_ + 1)],
                start=True, stop=True), wait=[t_brow, bank_free[4 + e_]], inc="pe")
            tvb = B.op("vector", lambda e, e_=e_: e.tensor_copy(
                beta_bc[:, 512 * e_:512 * (e_ + 1)], bank(4 + e_)), wait=[tmb], inc="ve")
            bank_free[4 + e_] = tvb
            bank_free[e_] = gb_toks[e_]
            gb_toks.append(tvb)

        t_out_last = None
        t_lnv_prev = t_lns_prev = None
        chunk_free = {}
        hblk_free = None
        moeh_free = None
        yb_free = {}
        ev_hist = {}
        for b in range(NB):
            hb_toks = {}
            for hc in range(8):
                hb_toks[hc] = B.dma(
                    "sync", hblk[:, 4096 * hc:4096 * (hc + 1)],
                    hs[b][:, 8 * hc:8 * (hc + 1), :].rearrange("p a s -> p (a s)"),
                    f"dhb{hc}", wait=[chunk_free.get(hc)] + hs_fin)
            if b > 0:
                emit_ln3(b - 1, ev_hist[b - 1])
            ev_done = {}
            moes_pb = moes.rearrange("(n p) d -> p n d", p=128)
            for eh in range(2):
                t_moeh = B.dma(
                    "scalar", moeh.ap().rearrange("p (t j) -> p t j", j=1024),
                    moes_pb[:, TPB * b:TPB * (b + 1), 1024 * eh:1024 * (eh + 1)],
                    "dmoe", wait=[moeh_free] + pb1_drain)
                moe_toks = [t_moeh]
                group_last = {}
                down_g = down_b.rearrange("(g a) p e -> g p a e", a=4)
                for ig in range(NI // 4):
                    dbf = ig % 2
                    t_dl = B.dma("sync", dbuf[dbf].ap().rearrange("p (a e) -> p a e", e=1024),
                                 down_g[ig][:, :, 1024 * eh:1024 * (eh + 1)],
                                 f"ddn{dbf}", wait=[conv_tok["down"], buf_free.get(("dbuf", dbf))] + pb1_drain)
                    tm = None
                    for ii in range(4):
                        i = 4 * ig + ii
                        for t in range(TPB):
                            for n_ in range(2):
                                bk = 2 * t + n_
                                is_last = (t == TPB - 1 and n_ == 1 and ii == 3)
                                tm = B.op("tensor", lambda e, i=i, ii=ii, t=t, n_=n_, bk=bk, dbf=dbf: e.matmul(
                                    bank(bk), hblk[:, 512 * i + 128 * t:512 * i + 128 * (t + 1)],
                                    dbuf[dbf][:, 1024 * ii + 512 * n_:1024 * ii + 512 * (n_ + 1)],
                                    start=(i == 0), stop=(i == NI - 1)),
                                    wait=[t_dl, hb_toks[i // 8], bank_free[bk] if i == 0 else None],
                                    inc="pe" if (is_last or i == NI - 1) else None)
                                if i == NI - 1:
                                    group_last[(t, n_)] = tm
                    buf_free[("dbuf", dbf)] = tm
                    if eh == 1 and ig % 2 == 1:
                        chunk_free[ig // 2] = tm
                for t in range(TPB):
                    for n_ in range(2):
                        bk = 2 * t + n_
                        off = 2048 * t + 1024 * eh + 512 * n_
                        tev = B.op("vector", lambda e, off=off, t=t, n_=n_, bk=bk, b=b: e.scalar_tensor_tensor(
                            ybuf2[b % 2][:, off:off + 512], bank(bk), 1.0,
                            moeh[:, 1024 * t + 512 * n_:1024 * t + 512 * (n_ + 1)],
                            ALU.mult, ALU.add),
                            wait=[group_last[(t, n_)], yb_free.get((b % 2, t))] + moe_toks, inc="ve")
                        bank_free[bk] = tev
                        ev_done[(t, eh, n_)] = tev
                moeh_free = ev_done[(TPB - 1, eh, 1)]
            def emit_ln3(bb, ev):
                nonlocal t_lnv_prev, t_lns_prev, t_out_last
                for t in range(TPB):
                    T = TPB * bb + t
                    y_sl = ybuf2[bb % 2][:, 2048 * t:2048 * (t + 1)]
                    rdy = [ev[(t, 0, 0)], ev[(t, 0, 1)],
                           ev[(t, 1, 0)], ev[(t, 1, 1)]]
                    t_s3 = B.op("vector", lambda e, y_sl=y_sl: e.reduce_sum(
                        small[:, 60:61], y_sl, axis=AX.X), wait=rdy + [t_lnv_prev], inc="ve")
                    t_q3 = B.op("scalar", lambda e, y_sl=y_sl: e.activation(
                        scr[:, :], y_sl, AF.Square, accum_out=small[:, 61:62]),
                        wait=rdy + [t_lnv_prev], inc="sc")
                    u1 = B.op("vector", lambda e: e.tensor_scalar(
                        small[:, 62:63], small[:, 60:61], 1.0 / D, None, ALU.mult),
                        wait=[t_s3], inc="ve")
                    u2 = B.op("vector", lambda e: e.tensor_scalar(
                        small[:, 63:64], small[:, 61:62], 1.0 / D, EPS_LN, ALU.mult, ALU.add),
                        wait=[t_q3], inc="ve")
                    u3 = B.op("vector", lambda e: e.tensor_tensor(
                        small[:, 64:65], small[:, 62:63], small[:, 62:63], ALU.mult),
                        wait=[u1], inc="ve")
                    u4 = B.op("vector", lambda e: e.tensor_tensor(
                        small[:, 65:66], small[:, 63:64], small[:, 64:65], ALU.subtract),
                        wait=[u2, u3], inc="ve")
                    u5 = B.op("scalar", lambda e: e.activation(
                        small[:, 66:67], small[:, 65:66], AF.Ln),
                        wait=[u4, t_lns_prev], inc="sc")
                    u6 = B.op("scalar", lambda e: e.activation(
                        small[:, 67:68], small[:, 66:67], AF.Exp, scale=-0.5),
                        wait=[u5], inc="sc")
                    u7 = B.op("vector", lambda e: e.scalar_tensor_tensor(
                        small[:, 68:69], small[:, 62:63], -1.0, small[:, 67:68],
                        ALU.mult, ALU.mult), wait=[u6], inc="ve")
                    t_lnv_prev = u7
                    t_yn = B.op("scalar", lambda e, y_sl=y_sl: e.activation(
                        ynb[:, :], y_sl, AF.Identity,
                        scale=small[:, 67:68], bias=small[:, 68:69]),
                        wait=[u7, buf_free.get("ynb")], inc="sc")
                    t_lns_prev = t_yn
                    yb_free[(bb % 2, t)] = t_yn
                    T_ = TPB * bb + t
                    x1k = B.dma("scalar", x1tl[t % 2][:, :], x1t[T_], f"dx1t{t % 2}",
                                    wait=[buf_free.get(("x1tl", t % 2))] + pb1_drain)
                    t_xb = B.op("gpsimd", lambda e, t=t: e.tensor_tensor(
                        xbb[:, :], x1tl[t % 2][:, :], beta_bc[:, :],
                        ALU.add), wait=[x1k] + gb_toks + [buf_free.get("xbb")],
                        inc="po")
                    buf_free[("x1tl", t % 2)] = t_xb
                    t_t1 = B.op("vector", lambda e: e.tensor_tensor(
                        ynb[:, :], ynb[:, :], gamma_bc[:, :], ALU.mult),
                        wait=[t_yn] + gb_toks, inc="ve")
                    t_o = B.op("vector", lambda e, t=t: e.tensor_tensor(
                        outsb[t % 2][:, :], ynb[:, :], xbb[:, :], ALU.add),
                        wait=[t_t1, t_xb, buf_free.get(("out", t % 2))], inc="ve")
                    buf_free["ynb"] = t_o
                    t_od = B.dma("scalar", outt[T], outsb[t % 2][:, :], f"dout{t % 2}",
                                     wait=[t_o])
                    buf_free[("out", t % 2)] = t_od
                    buf_free["xbb"] = t_o
                    t_out_last = t_od

            ev_hist[b] = dict(ev_done)

        emit_ln3(NB - 1, ev_hist[NB - 1])
        fin0, fin1 = B.tok("dout0"), B.tok("dout1")
        B.op("sync", lambda e, a=fin0, b_=fin1: (
            e.wait_ge(B.handles[a[0]], a[1]), e.wait_ge(B.handles[b_[0]], b_[1])))
        B.flush("pb2")
        pb2stk.close()

    return nc, B


def _prep_inputs(inputs, b):
    f32 = np.float32
    wq = np.asarray(inputs["L0_attn_w_q"], f32)
    wk = np.asarray(inputs["L0_attn_w_k"], f32)
    wv = np.asarray(inputs["L0_attn_w_v"], f32)
    wo = np.asarray(inputs["L0_attn_w_o"], f32)
    ew = np.asarray(inputs["L0_ffn_expert_weight"], f32)
    up = np.asarray(inputs["L0_ffn_up_w"], f32)
    dn = np.asarray(inputs["L0_ffn_down_w"], f32)
    mems = np.asarray(inputs["L0_attn_memories"], f32)
    mn = np.asarray(inputs["L0_attn_memory_norms"], f32)
    gw = np.asarray(inputs["L0_ffn_gate_w"], f32)

    def qlay(w):
        wT = np.ascontiguousarray(w.T)
        return np.ascontiguousarray(
            wT.reshape(16, 128, 16, 128).transpose(2, 1, 0, 3).reshape(16, 128, 2048))

    def clay(w):
        return np.ascontiguousarray(w.T.reshape(16, 128, 2048))

    m = {
        "wq_h": qlay(wq), "wk_h": clay(wk), "wv_h": clay(wv), "wo_h": clay(wo),
        "ew_hh": np.ascontiguousarray(ew.reshape(16, 128, 2048)),
        "up_h": np.ascontiguousarray(
            up.reshape(16, 128, 64, 128).transpose(2, 1, 0, 3).reshape(64, 128, 2048)),
        "down_hh": np.ascontiguousarray(dn.reshape(64, 128, 2048)),
        "mems_h": np.ascontiguousarray(mems.transpose(1, 2, 0, 3)),
        "mnT_h": np.ascontiguousarray(mn.transpose(1, 2, 0)),
        "gatew_h": np.ascontiguousarray(gw.reshape(16, 128, 8).transpose(1, 0, 2).reshape(128, 128)),
        "gate_h": np.asarray(inputs["L0_attn_gate"], f32).reshape(16, 1),
        "gamma_h": np.asarray(inputs["L0_ffn_ln_gamma"], f32).reshape(1, 2048),
        "beta_h": np.asarray(inputs["L0_ffn_ln_beta"], f32).reshape(1, 2048),
        "maskT_h": np.ascontiguousarray(np.triu(np.ones((128, 128), f32))),
        "identb_h": np.eye(128).astype(ml_dtypes.bfloat16),
        "x": np.ascontiguousarray(np.asarray(inputs["L0_attn_hidden_states"], f32)[b]),
    }
    return m


def kernel(**inputs):
    nc, _ = build_nc()
    in_maps = [_prep_inputs(inputs, b) for b in range(8)]
    res = run_bass_kernel_spmd(nc, in_maps, core_ids=list(range(8)))
    return np.stack([res.results[b]["out"] for b in range(8)], axis=0)
